# version 11
# baseline (speedup 1.0000x reference)
import numpy as np
import ml_dtypes
from contextlib import ExitStack
import concourse.bass as bass
import concourse.mybir as mybir
from concourse.bass_utils import run_bass_kernel_spmd
from concourse.ap import AP

F32 = mybir.dt.float32
BF16 = mybir.dt.bfloat16
I8 = mybir.dt.int8
I32 = mybir.dt.int32
ALU = mybir.AluOpType
ACT = mybir.ActivationFunctionType
AXL = mybir.AxisListType

L = 8192
DM = 1024
DI = 2048
CD = 2080
DIP = 4160
SPAN = 512
NSPAN = L // SPAN
PPS = SPAN // 128
ESLICES = [(0, 8), (8, 16), (16, 24), (24, 33)]
EW = 9 * 128
RG = [list(range(8))]
DBG = False


def bf(x):
    return np.asarray(x, dtype=ml_dtypes.bfloat16)


def build_nc():
    nc = bass.Bass("TRN2", target_bir_lowering=False, detect_race_conditions=False)

    x_fm = nc.dram_tensor("x_fm", [DM, L], F32, kind="ExternalInput").ap()
    w_inT = nc.dram_tensor("w_inT", [DM, DIP], F32, kind="ExternalInput").ap()
    w_outT = nc.dram_tensor("w_outT", [DI, DM], F32, kind="ExternalInput").ap()
    wqT = nc.dram_tensor("wqT", [DM, DM], I8, kind="ExternalInput").ap()
    cbT = nc.dram_tensor("cbT", [128, 17], F32, kind="ExternalInput").ap()
    cw = nc.dram_tensor("cw", [128, 68], F32, kind="ExternalInput").ap()
    dtb = nc.dram_tensor("dtb", [32, 1], F32, kind="ExternalInput").ap()
    alog = nc.dram_tensor("alog", [32, 1], F32, kind="ExternalInput").ap()
    dvec = nc.dram_tensor("dvec", [1, 32], F32, kind="ExternalInput").ap()
    nw16 = nc.dram_tensor("nw16", [128, 16], F32, kind="ExternalInput").ap()
    sw8 = nc.dram_tensor("sw8", [128, 8], F32, kind="ExternalInput").ap()
    fb8 = nc.dram_tensor("fb8", [128, 8], F32, kind="ExternalInput").ap()
    out_fm = nc.dram_tensor("out_fm", [DM, L], F32, kind="ExternalOutput").ap()

    WBF = nc.dram_tensor("WBF", [8, 128, DIP], BF16).ap()
    WOB = nc.dram_tensor("WOB", [8, 128, DI], BF16).ap()
    XN = nc.dram_tensor("XNd", [NSPAN, 128, 4096], BF16).ap()
    ACSB = nc.dram_tensor("ACSB", [2, 4096], F32).ap()
    CCIN = nc.dram_tensor("CCIN", [1, 32], F32).ap()
    CCOUT = nc.dram_tensor("CCOUT", [1, 32], F32).ap()
    dbg = {}
    if DBG:
        for nm, shp in [("dbg_xn", [128, 4096]), ("dbg_sz", [128, 8192]),
                        ("dbg_xh", [128, 8192]), ("dbg_bca", [32, 512]),
                        ("dbg_dt", [32, 512]), ("dbg_mdec", [128, 4096]),
                        ("dbg_y2", [128, 8192]), ("dbg_attn", [128, 4096]),
                        ("dbg_mx", [128, 1]), ("dbg_cc", [1, 32]),
                        ("dbg_mxcols", [128, NSPAN]),
                        ("dbg_srep", [128, 1]), ("dbg_sinv", [128, 1]),
                        ("dbg_swx", [128, 8]), ("dbg_q", [128, 4096]),
                        ("dbg_xp", [128, 4096]), ("dbg_f", [128, 4096])]:
            dbg[nm] = nc.dram_tensor(nm, shp, F32, kind="ExternalOutput").ap()

    eye_b = nc.inline_tensor(bf(np.eye(128)), "eye_b").ap()
    eye_f = nc.inline_tensor(np.eye(128, dtype=np.float32), "eye_f").ap()
    t2blk = np.zeros((128, 128), dtype=np.float32)
    t2blk[:64, :64] = np.triu(np.ones((64, 64), dtype=np.float32))
    t2blk[64:, 64:] = np.triu(np.ones((64, 64), dtype=np.float32))
    T2F = nc.inline_tensor(t2blk, "T2F").ap()
    bones = np.zeros((128, 128), dtype=np.float32)
    bones[:64, :64] = 1.0
    bones[64:, 64:] = 1.0
    BONES = nc.inline_tensor(bones, "BONES").ap()
    sel = np.zeros((4, 32, 48), dtype=np.float32)
    for i in range(16):
        sel[0, 16 + i, i] = 1.0
        sel[1, 16 + i, 32 + i] = 1.0
        sel[2, i, i] = 1.0
        sel[3, i, 32 + i] = 1.0
    SEL = nc.inline_tensor(bf(sel.transpose(1, 0, 2).reshape(32, 192)), "SEL").ap()

    es = ExitStack()
    with es:
        sb = lambda name, shape, dt: es.enter_context(nc.sbuf_tensor(name, shape, dt))
        ps = lambda name, shape, dt: es.enter_context(nc.psum_tensor(name, shape, dt))
        sem = lambda name: es.enter_context(nc.semaphore(name))

        Wsl = [sb(f"Wsl{i}", [128, 8 * EW], BF16) for i in range(2)]
        Wob = sb("Wob", [128, 2048], BF16)
        x_pk = sb("x_pk", [128, 4096], F32)
        xn_pk = sb("xn_pk", [128, 4096], BF16)
        scr_f = sb("scr_f", [128, 4096], F32)
        sz_pk = sb("sz_pk", [128, 8192], BF16)
        xh_pk = sb("xh_pk", [128, 8192], BF16)
        y2_pk = sb("y2_pk", [128, 8192], BF16)
        xbc = sb("xbc", [128, 17 * 518], BF16)
        BCa = sb("BCa", [32, 512], BF16)
        dt_sb = sb("dt_sb", [64, 512], F32)
        dtT_sb = sb("dtT_sb", [128, 64], F32)
        acs_sb = sb("acs_sb", [128, 32], F32)
        ds_tmp = sb("ds_tmp", [128, 32], F32)
        G_sb = sb("G_sb", [128, 32], F32)
        dtds = sb("dtds", [128, 32], BF16)
        dtb16 = sb("dtb16", [128, 32], BF16)
        M3 = sb("M3", [128, 128], F32)
        Mdec = sb("Mdec", [128, 4096], BF16)
        Btd = sb("Btd", [128, 48], BF16)
        BfCf = sb("BfCf", [48, 256], BF16)
        lam = [sb(f"lam{i}", [16, 32], BF16) for i in range(2)]
        lam0 = sb("lamz", [16, 32], BF16)
        Rp = [sb(f"Rp{i}", [48, 2048], BF16) for i in range(2)]
        Rz1c = sb("Rz1c", [16, 2048], BF16)
        Rtmp = sb("Rtmp", [16, 2048], BF16)
        xh_tok = sb("xh_tok", [128, 2048], BF16)
        xdd = sb("xdd", [128, 2048], BF16)
        xhD = sb("xhD", [128, 2048], BF16)
        rowb = sb("rowb", [1, 2048], F32)
        rms_b = sb("rms_b", [128, 512], BF16)
        cb_sb = sb("cb_sb", [128, 17], F32)
        cw_sb = sb("cw_sb", [128, 68], F32)
        dtb_sb = sb("dtb_sb", [64, 1], F32)
        acol = sb("acol", [64, 1], F32)
        Drep = sb("Drep", [128, 32], BF16)
        nw_sb = sb("nw_sb", [128, 16], F32)
        sw_sb = sb("sw_sb", [128, 8], F32)
        fb_sb = sb("fb_sb", [128, 8], F32)
        swx = sb("swx", [128, 8], F32)
        mx_acc = sb("mx_acc", [128, 1], F32)
        mxcols = sb("mxcols", [128, NSPAN], F32)
        cc_sb = sb("cc_sb", [1, 32], F32)
        srep = sb("srep", [128, 1], F32)
        sinv = sb("sinv", [128, 1], F32)
        srow1 = sb("srow1", [1, 1], F32)
        ones_r = sb("ones_r", [1, 128], F32)
        ones_cb = sb("ones_cb", [128, 1], BF16)
        eyeb_sb = sb("eyeb_sb", [128, 128], BF16)
        eyef_sb = sb("eyef_sb", [128, 128], F32)
        t2_sb = sb("t2_sb", [128, 128], F32)
        bo_sb = sb("bo_sb", [128, 128], F32)
        sel_sb = sb("sel_sb", [32, 192], BF16)

        PA = ps("PA", [128, 512], F32)
        PB = ps("PB", [128, 512], F32)
        PY = ps("PY", [128, 1024], F32)
        PYO = ps("PYO", [128, 1024], F32)
        PT = ps("PT", [128, 2048], BF16)

        sLD = sem("sLD"); sST = sem("sST"); sGD = sem("sGD")
        sT = sem("sT"); sV = sem("sV"); sSC = sem("sSC"); sCC = sem("sCC")
        n = dict(ld=0, st=0, gd=0, t=0, v=0, sc=0, cc=0)

        def iT(inst): n['t'] += 1; return inst.then_inc(sT, 1)
        def iV(inst): n['v'] += 1; return inst.then_inc(sV, 1)
        def iS(inst): n['sc'] += 1; return inst.then_inc(sSC, 1)
        def iLD(inst): n['ld'] += 16; return inst.then_inc(sLD, 16)
        def iST(inst): n['st'] += 16; return inst.then_inc(sST, 16)
        def iGD(inst): n['gd'] += 16; return inst.then_inc(sGD, 16)

        def wv(e): e.wait_ge(sV, n['v'])
        def wt(e): e.wait_ge(sT, n['t'])
        def wsc(e): e.wait_ge(sSC, n['sc'])

        def b3(t, off, d0, d1, d2):
            a = t[:, :]
            return AP(a.tensor, a.offset + off, [[a.ap[0][0], d0[1]] if d0[0] is None else d0,
                                                 d1, d2])

        blk = es.enter_context(nc.Block())

        @blk.sync
        def _(sy):
            iLD(sy.dma_start(out=cb_sb[:, :], in_=cbT))
            iLD(sy.dma_start(out=cw_sb[:, :], in_=cw))
            iLD(sy.dma_start(out=dtb_sb[32:64, :], in_=dtb))
            iLD(sy.dma_start(out=acol[32:64, :], in_=alog))
            iLD(sy.dma_start(out=nw_sb[:, :], in_=nw16))
            iLD(sy.dma_start(out=sw_sb[:, :], in_=sw8))
            iLD(sy.dma_start(out=fb_sb[:, :], in_=fb8))
            iLD(sy.dma_start(out=eyeb_sb[:, :], in_=eye_b))
            iLD(sy.dma_start(out=eyef_sb[:, :], in_=eye_f))
            iLD(sy.dma_start(out=t2_sb[:, :], in_=T2F))
            iLD(sy.dma_start(out=bo_sb[:, :], in_=BONES))
            iLD(sy.dma_start(out=sel_sb[:, :], in_=SEL))
            iLD(sy.dma_start(out=cc_sb[:, :], in_=dvec))
        ld_pro = n['ld']

        @blk.gpsimd
        def _(g):
            iGD(g.dma_start(out=out_fm, in_=x_fm))
        gd_xcopy = n['gd']

        @blk.vector
        def _(v):
            v.wait_ge(sLD, ld_pro)
            v.memset(ones_r[:, :], 1.0)
            v.memset(ones_cb[:, :], 1.0)
            v.memset(Btd[:, :], 0.0)
            v.memset(Rp[0][:, :], 0.0)
            v.memset(Rp[1][:, :], 0.0)
            v.memset(lam[0][:, :], 0.0)
            v.memset(lam[1][:, :], 0.0)
            v.memset(mx_acc[:, :], 0.0)
            iV(v.memset(xbc[:, :], 0.0))
        v0 = n['v']

        @blk.scalar
        def _(s_):
            s_.wait_ge(sV, v0)
            iS(s_.activation(acol[32:64, :], acol[32:64, :], ACT.Exp))
        sc0 = n['sc']

        @blk.vector
        def _(v):
            v.wait_ge(sSC, sc0)
            iV(v.tensor_single_scalar(acol[32:64, :], acol[32:64, :], -1.0, ALU.mult))
        v1 = n['v']

        @blk.tensor
        def _(t):
            t.wait_ge(sV, v1)
            iT(t.matmul(PA[:, 0:32], ones_r[:, :], cc_sb[:, :], start=True, stop=True))
        t0c = n['t']

        @blk.vector
        def _(v):
            v.wait_ge(sT, t0c)
            iV(v.tensor_copy(Drep[:, :], PA[:, 0:32]))
        v2 = n['v']

        for k in range(8):
            @blk.gpsimd
            def _(g, k=k):
                if k == 0:
                    g.wait_ge(sV, v2)
                if k >= 2:
                    g.wait_ge(sST, (k - 1) * 16)
                buf = xh_pk if (k % 2 == 0) else y2_pk
                iGD(g.dma_start(out=buf[:, 0:DIP], in_=w_inT[k * 128:(k + 1) * 128, :]))
            @blk.sync
            def _(sy, k=k):
                sy.wait_ge(sGD, gd_xcopy + (k + 1) * 16)
                buf = xh_pk if (k % 2 == 0) else y2_pk
                iST(sy.dma_start(out=WBF[k], in_=buf[:, 0:DIP]))
        st_wbf = n['st']

        for i in range(16):
            @blk.gpsimd
            def _(g, i=i):
                if i == 0:
                    g.wait_ge(sST, st_wbf)
                if i >= 2:
                    g.wait_ge(sST, st_wbf + (i - 1) * 16)
                iGD(g.dma_start(out=xh_pk[:, (i % 2) * 1024:(i % 2) * 1024 + 1024],
                                in_=w_outT[i * 128:(i + 1) * 128, :]))
            @blk.vector
            def _(v, i=i):
                v.wait_ge(sGD, gd_xcopy + 128 + (i + 1) * 16)
                h = (i % 2) * 1024
                iV(v.tensor_scalar(xh_pk[:, h:h + 1024], xh_pk[:, h:h + 1024],
                                   nw_sb[:, i:i + 1], None, ALU.mult))
            @blk.sync
            def _(sy, i=i):
                sy.wait_ge(sV, v2 + i + 1)
                h = (i % 2) * 1024
                a = xh_pk[:, h:h + 1024]
                dst3 = AP(WOB.tensor, i * 128, [[DI, 128], [128 * DI, 8], [1, 128]])
                src3 = AP(a.tensor, a.offset, [[a.ap[0][0], 128], [128, 8], [1, 128]])
                iST(sy.dma_start(out=dst3, in_=src3))
        st_wob = n['st']

        for s in range(NSPAN):
            if s == 0:
                @blk.scalar
                def _(s_):
                    dst = b3(x_pk, 0, [None, 128], [512, 8], [1, 512])
                    src = AP(x_fm.tensor, 0, [[L, 128], [128 * L, 8], [1, 512]])
                    s_.dma_start(out=dst, in_=src).then_inc(sCC, 16)
                n['cc'] += 16
            @blk.sync
            def _(sy, s=s):
                if s == 0:
                    sy.wait_ge(sST, st_wob)
                    w_load0 = AP(WBF.tensor, 0, [[DIP, 128], [128 * DIP, 8], [1, 8 * 128]])
                    dst0 = b3(Wsl[0], 0, [None, 128], [EW, 8], [1, 8 * 128])
                    iLD(sy.dma_start(out=dst0, in_=w_load0))
                if s > 0:
                    wv(sy)
                    dst = b3(x_pk, 0, [None, 128], [512, 8], [1, 512])
                    src = AP(x_fm.tensor, s * 512, [[L, 128], [128 * L, 8], [1, 512]])
                    iLD(sy.dma_start(out=dst, in_=src))
            ld_x = n['ld']

            @blk.vector
            def _(v, s=s, ld_x=ld_x):
                if s == 0:
                    v.wait_ge(sCC, 16)
                else:
                    v.wait_ge(sLD, ld_x)
                wsc(v); wt(v)
                src = b3(xbc, 512, [None, 128], [518, 17], [1, 3])
                dst = b3(xbc, 0, [None, 128], [518, 17], [1, 3])
                iV(v.tensor_copy(dst, src))
                iV(v.tensor_copy(xh_pk[:, 4096:8192], x_pk[:, :]))
            vb = n['v']

            @blk.scalar
            def _(s_, vb=vb):
                s_.wait_ge(sV, vb)
                iS(s_.activation(xh_pk[:, 0:4096], xh_pk[:, 4096:8192], ACT.Square))
            scb = n['sc']

            @blk.tensor
            def _(t, scb=scb):
                t.wait_ge(sSC, scb)
                for k in range(8):
                    iT(t.matmul(PA[0:1, 0:512], ones_cb[:, :], xh_pk[:, 4096 + k * 512:4096 + (k + 1) * 512],
                                start=(k == 0), stop=(k == 7)))
                for k in range(8):
                    iT(t.matmul(PB[0:1, 0:512], ones_cb[:, :], xh_pk[:, k * 512:(k + 1) * 512],
                                start=(k == 0), stop=(k == 7)))
            tb = n['t']

            @blk.vector
            def _(v, tb=tb):
                v.wait_ge(sT, tb)
                iV(v.tensor_scalar(rowb[:, 0:512], PA[0:1, 0:512], 1.0 / DM, None, ALU.mult))
                iV(v.tensor_scalar(rowb[:, 512:1024], PB[0:1, 0:512], 1.0 / DM, 1e-5, ALU.mult, ALU.add))
                iV(v.tensor_tensor(rowb[:, 1536:2048], rowb[:, 0:512], rowb[:, 0:512], ALU.mult))
                iV(v.scalar_tensor_tensor(rowb[:, 1536:2048], rowb[:, 1536:2048], -1.0,
                                          rowb[:, 512:1024], ALU.mult, ALU.add))
                iV(v.reciprocal(rowb[:, 512:1024], rowb[:, 1536:2048]))
            vr1 = n['v']

            @blk.scalar
            def _(s_, vr1=vr1):
                s_.wait_ge(sV, vr1)
                iS(s_.activation(rowb[:, 1024:1536], rowb[:, 512:1024], ACT.Sqrt))
            scr = n['sc']

            @blk.vector
            def _(v, scr=scr):
                v.wait_ge(sSC, scr)
                iV(v.tensor_tensor(rowb[:, 1536:2048], rowb[:, 0:512], rowb[:, 1024:1536], ALU.mult))
            vr2 = n['v']

            @blk.tensor
            def _(t, vr2=vr2):
                t.wait_ge(sV, vr2)
                iT(t.matmul(PY[:, 0:512], ones_r[:, :], rowb[0:1, 1024:1536], start=True, stop=True))
                iT(t.matmul(PYO[:, 0:512], ones_r[:, :], rowb[0:1, 1536:2048], start=True, stop=True))
            tbc = n['t']

            @blk.vector
            def _(v, tbc=tbc, s=s):
                v.wait_ge(sT, tbc)
                a_ap = AP(PY[:, :].tensor, PY[:, :].offset, [[PY[:, :].ap[0][0], 128], [0, 8], [1, 512]])
                b_ap = AP(PYO[:, :].tensor, PYO[:, :].offset, [[PYO[:, :].ap[0][0], 128], [0, 8], [1, 512]])
                iV(v.tensor_tensor(scr_f[:, :], x_pk[:, :], a_ap, ALU.mult))
                iV(v.tensor_tensor(xn_pk[:, :], scr_f[:, :], b_ap, ALU.subtract))
            vxn = n['v']

            @blk.scalar
            def _(s_, vxn=vxn):
                s_.wait_ge(sV, vxn)
                iS(s_.activation(xh_pk[:, 0:4096], xn_pk[:, :], ACT.Abs))
            sc_abs = n['sc']
            @blk.vector
            def _(v, sc_abs=sc_abs, s=s):
                v.wait_ge(sSC, sc_abs)
                iV(v.tensor_reduce(mxcols[:, s:s + 1], xh_pk[:, 0:4096], AXL.X, ALU.max))

            @blk.sync
            def _(sy, s=s, vxn=vxn):
                sy.wait_ge(sV, vxn)
                iST(sy.dma_start(out=XN[s], in_=xn_pk[:, :]))
            if DBG and s == 0:
                @blk.gpsimd
                def _(g, vxn=vxn):
                    g.wait_ge(sV, vxn)
                    iGD(g.dma_start(out=dbg["dbg_xn"], in_=xn_pk[:, :]))


            evac_cnt = {}
            for esl in range(4):
                e0, e1 = ESLICES[esl]
                @blk.sync
                def _(sy, s=s, esl=esl):
                    sy.wait_ge(sT, n['t'])
                    nsl = esl + 1
                    if nsl < 4:
                        cols = min(ESLICES[nsl][1] * 128, DIP) - ESLICES[nsl][0] * 128
                        dst = b3(Wsl[nsl % 2], 0, [None, 128], [EW, 8], [1, cols])
                        src = AP(WBF.tensor, ESLICES[nsl][0] * 128,
                                 [[DIP, 128], [128 * DIP, 8], [1, cols]])
                        iLD(sy.dma_start(out=dst, in_=src))
                    elif s + 1 < NSPAN:
                        dst = b3(Wsl[0], 0, [None, 128], [EW, 8], [1, 8 * 128])
                        src = AP(WBF.tensor, 0, [[DIP, 128], [128 * DIP, 8], [1, 8 * 128]])
                        iLD(sy.dma_start(out=dst, in_=src))
                wneed = n['ld']

                @blk.tensor
                def _(t, esl=esl, e0=e0, e1=e1, wneed=wneed, vb=vb, vxn=vxn):
                    t.wait_ge(sLD, wneed - 16)
                    if esl == 0:
                        t.wait_ge(sV, max(vb, vxn))
                    for e in range(e0, e1):
                        pp = PA if (e % 2 == 0) else PB
                        if e - 2 >= 0:
                            t.wait_ge(sSC, evac_cnt[e - 2])
                        mrows = 128 if e < 32 else 64
                        for k in range(8):
                            iT(t.matmul(pp[0:mrows, 0:512],
                                        Wsl[esl % 2][:, k * EW + (e - e0) * 128:k * EW + (e - e0) * 128 + mrows],
                                        xn_pk[:, k * 512:(k + 1) * 512],
                                        start=(k == 0), stop=(k == 7)))
                        tcnt = n['t']
                        if e < 16:
                            @blk.scalar
                            def _(s_, e=e, pp=pp, tcnt=tcnt):
                                s_.wait_ge(sT, tcnt)
                                iS(s_.activation(sz_pk[:, e * 512:(e + 1) * 512], pp[:, 0:512], ACT.Silu))
                        elif e < 32:
                            g = e - 16
                            @blk.scalar
                            def _(s_, g=g, pp=pp, tcnt=tcnt):
                                s_.wait_ge(sT, tcnt)
                                iS(s_.activation(xbc[:, g * 518 + 3:g * 518 + 515], pp[:, 0:512], ACT.Copy))
                        else:
                            @blk.scalar
                            def _(s_, pp=pp, tcnt=tcnt):
                                s_.wait_ge(sT, tcnt)
                                iS(s_.activation(xbc[0:32, 16 * 518 + 3:16 * 518 + 515], pp[0:32, 0:512], ACT.Copy))
                        evac_cnt[e] = n['sc']
            sc_ip = n['sc']

            @blk.vector
            def _(v, sc_ip=sc_ip):
                v.wait_ge(sSC, sc_ip)
                iV(v.tensor_scalar(dt_sb[32:64, :], PA[32:64, 0:512], dtb_sb[32:64, :], None, ALU.add))
            v_dtb = n['v']

            @blk.scalar
            def _(s_, sc_ip=sc_ip, v_dtb=v_dtb):
                s_.wait_ge(sV, v_dtb)
                iS(s_.activation(scr_f[32:64, 0:512], dt_sb[32:64, :], ACT.Abs))
            @blk.scalar
            def _(s_):
                iS(s_.activation(scr_f[32:64, 512:1024], scr_f[32:64, 0:512], ACT.Exp, bias=0.0, scale=-1.0))
            @blk.scalar
            def _(s_):
                iS(s_.activation(scr_f[32:64, 512:1024], scr_f[32:64, 512:1024], ACT.Ln, bias=1.0, scale=1.0))
            sc_dt = n['sc']
            @blk.vector
            def _(v, sc_dt=sc_dt):
                v.wait_ge(sSC, sc_dt)
                iV(v.tensor_tensor(scr_f[32:64, 0:512], dt_sb[32:64, :], scr_f[32:64, 0:512], ALU.add))
                iV(v.scalar_tensor_tensor(dt_sb[32:64, :], scr_f[32:64, 0:512], 0.5,
                                          scr_f[32:64, 512:1024], ALU.mult, ALU.add))
            v_dt = n['v']

            if DBG and s == 0:
                @blk.gpsimd
                def _(g, v_dt=v_dt):
                    g.wait_ge(sV, v_dt)
                    wsc(g)
                    iGD(g.dma_start(out=dbg["dbg_sz"], in_=sz_pk[:, :]))
                    iGD(g.dma_start(out=dbg["dbg_dt"], in_=dt_sb[32:64, :]))

            for g in range(17):
                rr = 128 if g < 16 else 32
                dst = (xh_pk[:, g * 512:(g + 1) * 512] if g < 16 else BCa[:, :])
                @blk.vector
                def _(v, g=g, rr=rr, dst=dst, sc_ip=sc_ip):
                    if g == 0:
                        v.wait_ge(sSC, sc_ip)
                    d = dst if rr == 128 else dst
                    iV(v.tensor_scalar(d, xbc[0:rr, g * 518 + 0:g * 518 + 512],
                                       cw_sb[0:rr, g * 4:g * 4 + 1], None, ALU.mult))
                    for k in (1, 2, 3):
                        iV(v.scalar_tensor_tensor(d, xbc[0:rr, g * 518 + k:g * 518 + k + 512],
                                                  cw_sb[0:rr, g * 4 + k:g * 4 + k + 1], d,
                                                  ALU.mult, ALU.add))
                vcv = n['v']
                @blk.scalar
                def _(s_, g=g, rr=rr, dst=dst, vcv=vcv):
                    s_.wait_ge(sV, vcv)
                    iS(s_.activation(dst, dst, ACT.Silu, bias=cb_sb[0:rr, g:g + 1], scale=1.0))
            sc_cv = n['sc']

            if DBG and s == 0:
                @blk.gpsimd
                def _(g, sc_cv=sc_cv):
                    g.wait_ge(sSC, sc_cv)
                    iGD(g.dma_start(out=dbg["dbg_xh"], in_=xh_pk[:, :]))
                    iGD(g.dma_start(out=dbg["dbg_bca"], in_=BCa[:, :]))

            for p in range(PPS):
                gp = s * PPS + p
                pc = p * 128
                @blk.tensor
                def _(t, p=p, pc=pc, sc_cv=sc_cv):
                    wsc(t); wv(t)
                    iT(t.transpose(PT[:, 0:16], BCa[0:16, pc:pc + 128], eyeb_sb[0:16, 0:16]))
                    iT(t.matmul(PB[0:48, 256:320], sel_sb[:, 0:48], BCa[0:32, pc:pc + 64], start=True, stop=True))
                    iT(t.matmul(PB[0:48, 320:384], sel_sb[:, 48:96], BCa[0:32, pc + 64:pc + 128], start=True, stop=True))
                    iT(t.matmul(PB[0:48, 384:448], sel_sb[:, 96:144], BCa[0:32, pc:pc + 64], start=True, stop=True))
                    iT(t.matmul(PB[0:48, 448:512], sel_sb[:, 144:192], BCa[0:32, pc + 64:pc + 128], start=True, stop=True))
                    iT(t.transpose(PB[:, 0:32], dt_sb[32:64, pc:pc + 128], eyef_sb[32:64, 32:64]))
                tc1a = n['t']

                @blk.vector
                def _(v, pc=pc, tc1a=tc1a):
                    v.wait_ge(sT, tc1a)
                    iV(v.tensor_scalar(scr_f[32:64, pc:pc + 128],
                                       dt_sb[32:64, pc:pc + 128], acol[32:64, :], None, ALU.mult))
                    iV(v.tensor_copy(BfCf[:, 0:128], PB[0:48, 256:384]))
                    iV(v.tensor_copy(BfCf[:, 128:256], PB[0:48, 384:512]))
                    iV(v.tensor_copy(Btd[0:64, 0:16], PT[0:64, 0:16]))
                    iV(v.tensor_copy(Btd[64:128, 32:48], PT[64:128, 0:16]))
                    iV(v.tensor_copy(dtT_sb[:, 0:32], PB[:, 0:32]))
                vc1 = n['v']

                @blk.tensor
                def _(t, pc=pc, vc1=vc1):
                    t.wait_ge(sV, vc1)
                    iT(t.transpose(PB[:, 32:64], scr_f[32:64, pc:pc + 128], eyef_sb[32:64, 32:64]))
                tc1b = n['t']
                @blk.vector
                def _(v, tc1b=tc1b):
                    v.wait_ge(sT, tc1b)
                    iV(v.tensor_copy(dtT_sb[:, 32:64], PB[:, 32:64]))
                vc1b = n['v']

                @blk.tensor
                def _(t, vc1b=vc1b):
                    t.wait_ge(sV, vc1b)
                    iT(t.matmul(PB[:, 64:96], t2_sb[:, :], dtT_sb[:, 32:64], start=True, stop=True))
                    iT(t.matmul(PB[:, 96:128], bo_sb[:, :], dtT_sb[:, 32:64], start=True, stop=True))
                    iT(t.matmul(PB[:, 128:256], BfCf[:, 128:256], BfCf[:, 0:128], start=True, stop=True))
                tc2 = n['t']

                @blk.vector
                def _(v, tc2=tc2):
                    v.wait_ge(sT, tc2)
                    iV(v.tensor_copy(acs_sb[:, :], PB[:, 64:96]))
                vc2 = n['v']

                @blk.scalar
                def _(s_, vc2=vc2):
                    s_.wait_ge(sV, vc2)
                    iS(s_.activation(G_sb[:, :], PB[:, 64:96], ACT.Exp))
                    iS(s_.activation(lam0[:, :], PB[0:16, 96:128], ACT.Exp))
                sc_g = n['sc']

                @blk.vector
                def _(v, sc_g=sc_g):
                    v.wait_ge(sSC, sc_g)
                    iV(v.tensor_tensor(ds_tmp[:, :], PB[:, 96:128], acs_sb[:, :], ALU.subtract))
                vds = n['v']

                @blk.scalar
                def _(s_, vds=vds, gp=gp):
                    s_.wait_ge(sV, vds)
                    iS(s_.activation(dtds[:, :], ds_tmp[:, :], ACT.Exp))
                    iS(s_.activation(lam[gp % 2][:, :], PB[64:80, 96:128], ACT.Exp))
                sc_ds = n['sc']

                @blk.vector
                def _(v, sc_ds=sc_ds):
                    v.wait_ge(sSC, sc_ds)
                    iV(v.tensor_copy(dtb16[:, :], dtT_sb[:, 0:32]))
                    iV(v.tensor_tensor(M3[:, :], PB[:, 128:256], t2_sb[:, :], ALU.mult))
                vm3 = n['v']

                @blk.sync
                def _(sy, gp=gp, vc2=vc2):
                    sy.wait_ge(sV, vc2)
                    iST(sy.dma_start(out=ACSB[gp % 2], in_=acs_sb[:, :]))
                st_acs = n['st']

                @blk.tensor
                def _(t, p=p, vc1=vc1):
                    t.wait_ge(sV, vc1)
                    for k in range(16):
                        iT(t.transpose(PT[:, k * 128:(k + 1) * 128],
                                       xh_pk[:, k * 512 + p * 128:k * 512 + p * 128 + 128],
                                       eyeb_sb[:, :]))
                t_xh = n['t']
                @blk.vector
                def _(v, t_xh=t_xh):
                    v.wait_ge(sT, t_xh)
                    iV(v.tensor_copy(xh_tok[:, :], PT[:, :]))
                v_xh = n['v']

                @blk.vector
                def _(v, v_xh=v_xh, vm3=vm3):
                    v.wait_ge(sV, vm3)
                    dd = dtds[:, :]
                    ds_ap = AP(dd.tensor, dd.offset, [[dd.ap[0][0], 128], [1, 32], [0, 64]])
                    db = dtb16[:, :]
                    dt_ap = AP(db.tensor, db.offset, [[db.ap[0][0], 128], [1, 32], [0, 64]])
                    dr = Drep[:, :]
                    drep_ap = AP(dr.tensor, dr.offset, [[dr.ap[0][0], 128], [1, 32], [0, 64]])
                    iV(v.tensor_tensor(xhD[:, :], xh_tok[:, :], drep_ap, ALU.mult))
                    iV(v.tensor_tensor(xh_tok[:, :], xh_tok[:, :], dt_ap, ALU.mult))
                    iV(v.tensor_tensor(xdd[:, :], xh_tok[:, :], ds_ap, ALU.mult))
                v_xdd = n['v']

                for hq in range(2):
                    @blk.sync
                    def _(sy, gp=gp, hq=hq, st_acs=st_acs):
                        sy.wait_ge(sST, st_acs)
                        wv(sy)
                        src = AP(ACSB.tensor, (gp % 2) * 4096 + hq * 2048, [[2048, 1], [1, 2048]])
                        iLD(sy.dma_start(out=rowb[:, :], in_=src))
                    ld_row = n['ld']
                    for qq in range(2):
                        q = hq * 2 + qq
                        pq = PY if (qq == 0) else PYO
                        @blk.tensor
                        def _(t, q=q, qq=qq, pq=pq, ld_row=ld_row, vm3=vm3, v_xdd=v_xdd):
                            t.wait_ge(sLD, ld_row)
                            t.wait_ge(sV, max(vm3, v_xdd))
                            wv(t)
                            iT(t.matmul(pq[:, 0:512], ones_r[:, :], rowb[0:1, qq * 1024:qq * 1024 + 512],
                                        start=True, stop=True))
                            iT(t.matmul(pq[:, 512:1024], ones_r[:, :], rowb[0:1, qq * 1024 + 512:(qq + 1) * 1024],
                                        start=True, stop=True))
                        t_b2 = n['t']
                        @blk.vector
                        def _(v, pq=pq, t_b2=t_b2):
                            v.wait_ge(sT, t_b2)
                            aa = acs_sb[:, :]
                            acs_l0 = AP(aa.tensor, aa.offset, [[aa.ap[0][0], 128], [0, 32], [1, 32]])
                            iV(v.scalar_tensor_tensor(scr_f[:, 0:1024], acs_l0, -1.0, pq[:, :],
                                                      ALU.mult, ALU.add))
                            iV(v.tensor_single_scalar(scr_f[:, 0:1024], scr_f[:, 0:1024], 0.0, ALU.min))
                        v_d = n['v']
                        @blk.scalar
                        def _(s_, v_d=v_d):
                            s_.wait_ge(sV, v_d)
                            iS(s_.activation(scr_f[:, 1024:2048], scr_f[:, 0:1024], ACT.Exp))
                        sc_e = n['sc']
                        @blk.vector
                        def _(v, q=q, sc_e=sc_e):
                            v.wait_ge(sSC, sc_e)
                            mm = M3[:, :]
                            m3_ap = AP(mm.tensor, mm.offset + q * 32, [[mm.ap[0][0], 128], [1, 32], [0, 32]])
                            iV(v.tensor_tensor(Mdec[:, q * 1024:(q + 1) * 1024], scr_f[:, 1024:2048],
                                               m3_ap, ALU.mult))
                v_mdec = n['v']

                if DBG and gp == 0:
                    @blk.gpsimd
                    def _(g, v_mdec=v_mdec):
                        g.wait_ge(sV, v_mdec)
                        iGD(g.dma_start(out=dbg["dbg_mdec"], in_=Mdec[:, :]))

                @blk.tensor
                def _(t, v_mdec=v_mdec, v_xdd=v_xdd):
                    t.wait_ge(sV, max(v_mdec, v_xdd))
                    iT(t.matmul(PY[0:48, 0:512], Btd[:, :], xdd[:, 0:512], start=True, stop=True))
                    iT(t.matmul(PY[0:48, 512:1024], Btd[:, :], xdd[:, 512:1024], start=True, stop=True))
                    iT(t.matmul(PYO[0:48, 0:512], Btd[:, :], xdd[:, 1024:1536], start=True, stop=True))
                    iT(t.matmul(PYO[0:48, 512:1024], Btd[:, :], xdd[:, 1536:2048], start=True, stop=True))
                t_st = n['t']
                @blk.vector
                def _(v, gp=gp, t_st=t_st):
                    v.wait_ge(sT, t_st)
                    l0 = lam0[:, :]
                    lam0_ap = AP(l0.tensor, l0.offset, [[l0.ap[0][0], 16], [1, 32], [0, 64]])
                    iV(v.tensor_tensor(Rtmp[:, :], Rp[gp % 2][0:16, :], lam0_ap, ALU.mult))
                    iV(v.tensor_tensor(Rz1c[:, 0:1024], Rtmp[:, 0:1024], PY[0:16, :], ALU.add))
                    iV(v.tensor_tensor(Rz1c[:, 1024:2048], Rtmp[:, 1024:2048], PYO[0:16, :], ALU.add))
                    iV(v.tensor_copy(Rp[gp % 2][32:48, :], Rz1c[:, :]))
                    lp = lam[gp % 2][:, :]
                    lam_ap = AP(lp.tensor, lp.offset, [[lp.ap[0][0], 16], [1, 32], [0, 64]])
                    iV(v.tensor_tensor(Rtmp[:, :], Rz1c[:, :], lam_ap, ALU.mult))
                    iV(v.tensor_tensor(Rp[(gp + 1) % 2][0:16, 0:1024], Rtmp[:, 0:1024], PY[32:48, :], ALU.add))
                    iV(v.tensor_tensor(Rp[(gp + 1) % 2][0:16, 1024:2048], Rtmp[:, 1024:2048], PYO[32:48, :], ALU.add))
                v_r = n['v']

                for hh in range(2):
                    @blk.tensor
                    def _(t, hh=hh, gp=gp, v_r=v_r):
                        t.wait_ge(sV, n['v'])
                        for h2 in range(16):
                            h = hh * 16 + h2
                            md = Mdec[:, :]
                            md_ap = AP(md.tensor, md.offset + h, [[md.ap[0][0], 128], [32, 128]])
                            iT(t.matmul(PY[:, h2 * 64:(h2 + 1) * 64], md_ap,
                                        xh_tok[:, h * 64:(h + 1) * 64],
                                        start=(h2 % 8 == 0), stop=False))
                        iT(t.matmul(PY[:, 0:512], eyeb_sb[:, :], xhD[:, hh * 1024:hh * 1024 + 512],
                                    start=False, stop=True))
                        iT(t.matmul(PY[:, 512:1024], eyeb_sb[:, :], xhD[:, hh * 1024 + 512:(hh + 1) * 1024],
                                    start=False, stop=True))
                        iT(t.matmul(PYO[:, 0:512], BfCf[:, 0:128], Rp[gp % 2][:, hh * 1024:hh * 1024 + 512],
                                    start=True, stop=True))
                        iT(t.matmul(PYO[:, 512:1024], BfCf[:, 0:128],
                                    Rp[gp % 2][:, hh * 1024 + 512:(hh + 1) * 1024], start=True, stop=True))
                    t_y = n['t']
                    @blk.vector
                    def _(v, hh=hh, t_y=t_y):
                        v.wait_ge(sT, t_y)
                        gg = G_sb[:, :]
                        g_ap = AP(gg.tensor, gg.offset + hh * 16, [[gg.ap[0][0], 128], [1, 16], [0, 64]])
                        iV(v.tensor_tensor(scr_f[:, 1024:2048], PYO[:, :], g_ap, ALU.mult))
                        iV(v.tensor_tensor(xdd[:, hh * 1024:(hh + 1) * 1024], scr_f[:, 1024:2048],
                                           PY[:, :], ALU.add))
                v_y1 = n['v']

                @blk.tensor
                def _(t, v_y1=v_y1):
                    t.wait_ge(sV, v_y1)
                    for k in range(16):
                        iT(t.transpose(PT[:, k * 128:(k + 1) * 128],
                                       xdd[:, k * 128:(k + 1) * 128], eyeb_sb[:, :]))
                t_yt = n['t']
                @blk.vector
                def _(v, p=p, t_yt=t_yt):
                    v.wait_ge(sT, t_yt)
                    sz = sz_pk[:, :]
                    sz_ap = AP(sz.tensor, sz.offset + p * 128, [[sz.ap[0][0], 128], [512, 16], [1, 128]])
                    yy = y2_pk[:, :]
                    y2_ap = AP(yy.tensor, yy.offset + p * 128, [[yy.ap[0][0], 128], [512, 16], [1, 128]])
                    iV(v.tensor_tensor(y2_ap, PT[:, :], sz_ap, ALU.mult))

            if DBG and s == 0:
                @blk.gpsimd
                def _(g):
                    wv(g)
                    iGD(g.dma_start(out=dbg["dbg_y2"], in_=y2_pk[:, :]))
            dbg_gd = n['gd']
            @blk.scalar
            def _(s_):
                wv(s_); wt(s_)
                if DBG:
                    s_.wait_ge(sGD, dbg_gd)
                iS(s_.activation(xh_pk[:, 0:4096], y2_pk[:, 0:4096], ACT.Square))
                iS(s_.activation(xh_pk[:, 4096:8192], y2_pk[:, 4096:8192], ACT.Square))
            sc_sq = n['sc']
            @blk.tensor
            def _(t, sc_sq=sc_sq):
                t.wait_ge(sSC, sc_sq)
                for k in range(16):
                    iT(t.matmul(PA[0:1, 0:512], ones_cb[:, :], xh_pk[:, k * 512:(k + 1) * 512],
                                start=(k == 0), stop=(k == 15)))
            t_rms = n['t']
            @blk.vector
            def _(v, t_rms=t_rms):
                v.wait_ge(sT, t_rms)
                iV(v.tensor_scalar(rowb[:, 0:512], PA[0:1, 0:512], 1.0 / DI, 1e-5, ALU.mult, ALU.add))
                iV(v.reciprocal(rowb[:, 512:1024], rowb[:, 0:512]))
            v_rms1 = n['v']
            @blk.scalar
            def _(s_, v_rms1=v_rms1):
                s_.wait_ge(sV, v_rms1)
                iS(s_.activation(rowb[:, 1024:1536], rowb[:, 512:1024], ACT.Sqrt))
            sc_rms = n['sc']
            @blk.tensor
            def _(t, sc_rms=sc_rms):
                t.wait_ge(sSC, sc_rms)
                iT(t.matmul(PB[:, 0:512], ones_r[:, :], rowb[0:1, 1024:1536], start=True, stop=True))
            t_rb = n['t']
            @blk.vector
            def _(v, t_rb=t_rb):
                v.wait_ge(sT, t_rb)
                iV(v.tensor_copy(rms_b[:, :], PB[:, 0:512]))
                rr = rms_b[:, :]
                rms_ap = AP(rr.tensor, rr.offset, [[rr.ap[0][0], 128], [0, 16], [1, 512]])
                iV(v.tensor_tensor(y2_pk[:, :], y2_pk[:, :], rms_ap, ALU.mult))
            v_y3 = n['v']

            ev7 = {}
            for d in range(8):
                @blk.sync
                def _(sy, d=d, s=s):
                    sy.wait_ge(sT, n['t'])
                    src = AP(WOB.tensor, d * 128 * DI, [[DI, 128], [1, 2048]])
                    iLD(sy.dma_start(out=Wob[:, :], in_=src))
                ld_wo = n['ld']
                pp = PA if (d % 2 == 0) else PB
                @blk.tensor
                def _(t, d=d, pp=pp, v_y3=v_y3, ld_wo=ld_wo):
                    t.wait_ge(sV, v_y3)
                    t.wait_ge(sLD, ld_wo)
                    if d >= 2:
                        t.wait_ge(sV, ev7[d - 2])
                    for i in range(16):
                        iT(t.matmul(pp[:, 0:512], Wob[:, i * 128:(i + 1) * 128],
                                    y2_pk[:, i * 512:(i + 1) * 512],
                                    start=(i == 0), stop=(i == 15)))
                t_op = n['t']
                @blk.vector
                def _(v, d=d, pp=pp, t_op=t_op):
                    v.wait_ge(sT, t_op)
                    iV(v.tensor_copy(scr_f[:, d * 512:(d + 1) * 512], pp[:, 0:512]))
                ev7[d] = n['v']
            v_att = n['v']

            if DBG and s == 0:
                @blk.sync
                def _(sy, v_att=v_att):
                    sy.wait_ge(sV, v_att)
                    iST(sy.dma_start(out=dbg["dbg_attn"], in_=scr_f[:, :]))
            @blk.gpsimd
            def _(g, s=s, v_att=v_att):
                g.wait_ge(sV, v_att)
                if s == 0:
                    g.wait_ge(sGD, gd_xcopy)
                    if DBG:
                        g.wait_ge(sST, n['st'])
                dst = AP(out_fm.tensor, s * 512, [[L, 128], [128 * L, 8], [1, 512]])
                a = scr_f[:, :]
                src = AP(a.tensor, a.offset, [[a.ap[0][0], 128], [512, 8], [1, 512]])
                iGD(g.dma_start(out=dst, in_=src, accum_op=ALU.add))
        gd_p1 = n['gd']
        v_p1 = n['v']; t_p1 = n['t']; sc_p1 = n['sc']; st_xn_all = n['st']

        @blk.gpsimd
        def _(g):
            for k in range(8):
                iGD(g.dma_start(out=Wsl[0][:, k * EW:k * EW + 1024], in_=wqT[k * 128:(k + 1) * 128, :]))
        gd_wq = n['gd']

        @blk.vector
        def _(v):
            iV(v.tensor_reduce(mx_acc[:, 0:1], mxcols[:, 0:NSPAN], AXL.X, ALU.max))
        @blk.tensor
        def _(t):
            wv(t); wsc(t)
            iT(t.transpose(PB[0:1, 0:128], mx_acc[:, :], eyef_sb[:, :]))
        t_mx = n['t']
        @blk.vector
        def _(v, t_mx=t_mx):
            v.wait_ge(sT, t_mx)
            v.memset(cc_sb[:, :], 0.0)
            iV(v.tensor_reduce(cc_sb[0:1, 0:1], PB[0:1, 0:128], AXL.X, ALU.max))
        v_mx = n['v']
        @blk.gpsimd
        def _(g, v_mx=v_mx):
            g.wait_ge(sV, v_mx)
            iGD(g.dma_start(out=CCIN, in_=cc_sb[:, :]))
            g.wait_ge(sGD, n['gd'])
            n['cc'] += 1
            g.collective_compute("AllReduce", ALU.max, replica_groups=RG,
                                 ins=[CCIN.opt()], outs=[CCOUT.opt()]).then_inc(sCC, 1)
            g.wait_ge(sCC, n['cc'])
            iGD(g.dma_start(out=cc_sb[:, :], in_=CCOUT))
        gd_cc = n['gd']
        @blk.vector
        def _(v, gd_cc=gd_cc):
            v.wait_ge(sGD, gd_cc)
            iV(v.tensor_scalar(srow1[:, :], cc_sb[0:1, 0:1], float(np.float32(1.0) / np.float32(127.0)),
                               1e-5, ALU.mult, ALU.max))
        v_s = n['v']
        @blk.tensor
        def _(t, v_s=v_s):
            t.wait_ge(sV, v_s)
            iT(t.matmul(PB[:, 0:1], ones_r[:, :], srow1[:, :], start=True, stop=True))
        t_s = n['t']
        @blk.vector
        def _(v, t_s=t_s):
            v.wait_ge(sT, t_s)
            iV(v.tensor_copy(srep[:, :], PB[:, 0:1]))
            iV(v.tensor_scalar(swx[:, :], sw_sb[:, :], PB[:, 0:1], None, ALU.mult))
            iV(v.reciprocal(sinv[:, :], PB[:, 0:1]))
        v_srep = n['v']

        if DBG:
            @blk.sync
            def _(sy, v_mx=v_mx, v_srep=v_srep):
                sy.wait_ge(sV, v_mx)
                iST(sy.dma_start(out=dbg["dbg_mx"], in_=mx_acc[:, :]))
                iST(sy.dma_start(out=dbg["dbg_mxcols"], in_=mxcols[:, :]))
                sy.wait_ge(sV, v_srep)
                iST(sy.dma_start(out=dbg["dbg_cc"], in_=cc_sb[:, :]))
                iST(sy.dma_start(out=dbg["dbg_srep"], in_=srep[:, :]))
                iST(sy.dma_start(out=dbg["dbg_sinv"], in_=sinv[:, :]))
                iST(sy.dma_start(out=dbg["dbg_swx"], in_=swx[:, :]))

        v_q_hist = {}
        gd_fstore = {}
        for s2 in range(NSPAN):
            qsl = (xn_pk[:, :] if s2 % 2 == 0 else sz_pk[:, 0:4096])
            @blk.sync
            def _(sy, s2=s2, qsl=qsl):
                if s2 <= 1:
                    sy.wait_ge(sV, v_p1); sy.wait_ge(sT, t_p1)
                    sy.wait_ge(sSC, sc_p1); sy.wait_ge(sST, st_xn_all)
                else:
                    sy.wait_ge(sV, v_q_hist[s2 - 2])
                iLD(sy.dma_start(out=qsl, in_=XN[s2]))
            ld_f = n['ld']
            @blk.vector
            def _(v, ld_f=ld_f, s2=s2, qsl=qsl):
                v.wait_ge(sLD, ld_f)
                if s2 == 0:
                    v.wait_ge(sGD, gd_wq)
                if s2 >= 1:
                    v.wait_ge(sGD, gd_fstore[s2 - 1])
                wt(v)
                iV(v.tensor_scalar(scr_f[:, :], qsl, sinv[:, :], 8388608.0,
                                   ALU.mult, ALU.add))
                iV(v.tensor_single_scalar(y2_pk[:, 0:4096], scr_f[:, :], 8388608.0,
                                          ALU.subtract))
            v_q = n['v']
            v_q_hist[s2] = v_q
            if DBG and s2 == 0:
                @blk.gpsimd
                def _(g, v_q=v_q, ld_f=ld_f):
                    g.wait_ge(sV, v_q)
                    g.wait_ge(sLD, ld_f)
                    iGD(g.dma_start(out=dbg["dbg_q"], in_=y2_pk[:, 0:4096]))
            ffn_ev = {}
            for nn in range(8):
                pp = PA if (nn % 2 == 0) else PB
                @blk.tensor
                def _(t, nn=nn, pp=pp, v_q=v_q):
                    t.wait_ge(sV, v_q)
                    if nn >= 2:
                        t.wait_ge(sV, ffn_ev[nn - 2])
                    for k in range(8):
                        iT(t.matmul(pp[:, 0:512], Wsl[0][:, k * EW + nn * 128:k * EW + nn * 128 + 128],
                                    y2_pk[:, k * 512:(k + 1) * 512],
                                    start=(k == 0), stop=(k == 7)))
                t_f = n['t']
                @blk.vector
                def _(v, nn=nn, pp=pp, t_f=t_f):
                    v.wait_ge(sT, t_f)
                    iV(v.tensor_scalar(scr_f[:, nn * 512:(nn + 1) * 512], pp[:, 0:512],
                                       swx[:, nn:nn + 1], fb_sb[:, nn:nn + 1], ALU.mult, ALU.add))
                ffn_ev[nn] = n['v']
            @blk.gpsimd
            def _(g, s2=s2, last_ev=ffn_ev[7]):
                g.wait_ge(sV, last_ev)
                dst = AP(out_fm.tensor, s2 * 512, [[L, 128], [128 * L, 8], [1, 512]])
                a = scr_f[:, :]
                src = AP(a.tensor, a.offset, [[a.ap[0][0], 128], [512, 8], [1, 512]])
                iGD(g.dma_start(out=dst, in_=src, accum_op=ALU.add))
            gd_fstore[s2] = n['gd']

        @blk.sync
        def _(sy):
            sy.wait_ge(sST, n['st'])
            sy.wait_ge(sGD, n['gd'])

    return nc


_NC_CACHE = [None]


def kernel(L0_attn_x, L0_attn_W_in, L0_attn_conv_w, L0_attn_conv_b, L0_attn_dt_bias,
           L0_attn_A_log, L0_attn_D, L0_attn_norm_w, L0_attn_W_out,
           L0_ffn_weight_int8, L0_ffn_weight_scale, L0_ffn_bias):
    x = np.asarray(L0_attn_x, dtype=np.float32)
    w_inT = np.ascontiguousarray(np.asarray(L0_attn_W_in, dtype=np.float32).T)
    w_outT = np.ascontiguousarray(np.asarray(L0_attn_W_out, dtype=np.float32).T)
    wqT = np.ascontiguousarray(np.asarray(L0_ffn_weight_int8, dtype=np.int8).T)
    cb = np.zeros((17 * 128,), dtype=np.float32)
    cb[:CD] = np.asarray(L0_attn_conv_b, dtype=np.float32)
    cbT = np.ascontiguousarray(cb.reshape(17, 128).T)
    cwp = np.zeros((17 * 128, 4), dtype=np.float32)
    cwp[:CD] = np.asarray(L0_attn_conv_w, dtype=np.float32)
    cwh = np.ascontiguousarray(cwp.reshape(17, 128, 4).transpose(1, 0, 2).reshape(128, 68))
    dtbh = np.ascontiguousarray(np.asarray(L0_attn_dt_bias, dtype=np.float32).reshape(32, 1))
    alogh = np.ascontiguousarray(np.asarray(L0_attn_A_log, dtype=np.float32).reshape(32, 1))
    dvech = np.ascontiguousarray(np.asarray(L0_attn_D, dtype=np.float32).reshape(1, 32))
    nw16h = np.ascontiguousarray(np.asarray(L0_attn_norm_w, dtype=np.float32).reshape(16, 128).T)
    sw8h = np.ascontiguousarray(np.asarray(L0_ffn_weight_scale, dtype=np.float32).reshape(8, 128).T)
    fb8h = np.ascontiguousarray(np.asarray(L0_ffn_bias, dtype=np.float32).reshape(8, 128).T)

    if _NC_CACHE[0] is None:
        _NC_CACHE[0] = build_nc()
    nc = _NC_CACHE[0]

    in_maps = []
    for b in range(8):
        in_maps.append(dict(
            x_fm=np.ascontiguousarray(x[b].T),
            w_inT=w_inT, w_outT=w_outT, wqT=wqT, cbT=cbT, cw=cwh,
            dtb=dtbh, alog=alogh, dvec=dvech, nw16=nw16h, sw8=sw8h, fb8=fb8h,
        ))
    res = run_bass_kernel_spmd(nc, in_maps, core_ids=list(range(8)))
    out = np.empty((8, L, DM), dtype=np.float32)
    for b in range(8):
        out[b] = res.results[b]["out_fm"].T
    return out


if __name__ == "__main__":
    import reference as _r
    inp = _r.setup_inputs()
    o = kernel(**{k: np.asarray(v) for k, v in inp.items()})
    ref = np.asarray(_r.reference(**inp))
    err = np.abs(o - ref)
    rvr = float((err ** 2).mean() / (np.asarray(ref, dtype=np.float64) ** 2).mean())
    print("rvr", rvr, "maxabs", float(err.max()))


# version 12
# speedup vs baseline: 1.0055x; 1.0055x over previous
import numpy as np
import ml_dtypes
from contextlib import ExitStack
import concourse.bass as bass
import concourse.mybir as mybir
from concourse.bass_utils import run_bass_kernel_spmd
from concourse.ap import AP

F32 = mybir.dt.float32
BF16 = mybir.dt.bfloat16
I8 = mybir.dt.int8
I32 = mybir.dt.int32
ALU = mybir.AluOpType
ACT = mybir.ActivationFunctionType
AXL = mybir.AxisListType

L = 8192
DM = 1024
DI = 2048
CD = 2080
DIP = 4160
SPAN = 512
NSPAN = L // SPAN
PPS = SPAN // 128
ESLICES = [(0, 8), (8, 16), (16, 24), (24, 33)]
EW = 9 * 128
RG = [list(range(8))]
DBG = False


def bf(x):
    return np.asarray(x, dtype=ml_dtypes.bfloat16)


def build_nc():
    nc = bass.Bass("TRN2", target_bir_lowering=False, detect_race_conditions=False)

    x_fm = nc.dram_tensor("x_fm", [DM, L], F32, kind="ExternalInput").ap()
    w_inT = nc.dram_tensor("w_inT", [DM, DIP], F32, kind="ExternalInput").ap()
    w_outT = nc.dram_tensor("w_outT", [DI, DM], F32, kind="ExternalInput").ap()
    wqT = nc.dram_tensor("wqT", [DM, DM], I8, kind="ExternalInput").ap()
    cbT = nc.dram_tensor("cbT", [128, 17], F32, kind="ExternalInput").ap()
    cw = nc.dram_tensor("cw", [128, 68], F32, kind="ExternalInput").ap()
    dtb = nc.dram_tensor("dtb", [32, 1], F32, kind="ExternalInput").ap()
    alog = nc.dram_tensor("alog", [32, 1], F32, kind="ExternalInput").ap()
    dvec = nc.dram_tensor("dvec", [1, 32], F32, kind="ExternalInput").ap()
    nw16 = nc.dram_tensor("nw16", [128, 16], F32, kind="ExternalInput").ap()
    sw8 = nc.dram_tensor("sw8", [128, 8], F32, kind="ExternalInput").ap()
    fb8 = nc.dram_tensor("fb8", [128, 8], F32, kind="ExternalInput").ap()
    out_fm = nc.dram_tensor("out_fm", [DM, L], F32, kind="ExternalOutput").ap()

    WBF = nc.dram_tensor("WBF", [8, 128, DIP], BF16).ap()
    WOB = nc.dram_tensor("WOB", [8, 128, DI], BF16).ap()
    XN = nc.dram_tensor("XNd", [NSPAN, 128, 4096], BF16).ap()
    ACSB = nc.dram_tensor("ACSB", [2, 4096], F32).ap()
    CCIN = nc.dram_tensor("CCIN", [1, 32], F32).ap()
    CCOUT = nc.dram_tensor("CCOUT", [1, 32], F32).ap()
    dbg = {}
    if DBG:
        for nm, shp in [("dbg_xn", [128, 4096]), ("dbg_sz", [128, 8192]),
                        ("dbg_xh", [128, 8192]), ("dbg_bca", [32, 512]),
                        ("dbg_dt", [32, 512]), ("dbg_mdec", [128, 4096]),
                        ("dbg_y2", [128, 8192]), ("dbg_attn", [128, 4096]),
                        ("dbg_mx", [128, 1]), ("dbg_cc", [1, 32]),
                        ("dbg_mxcols", [128, NSPAN]),
                        ("dbg_srep", [128, 1]), ("dbg_sinv", [128, 1]),
                        ("dbg_swx", [128, 8]), ("dbg_q", [128, 4096]),
                        ("dbg_xp", [128, 4096]), ("dbg_f", [128, 4096])]:
            dbg[nm] = nc.dram_tensor(nm, shp, F32, kind="ExternalOutput").ap()

    eye_b = nc.inline_tensor(bf(np.eye(128)), "eye_b").ap()
    eye_f = nc.inline_tensor(np.eye(128, dtype=np.float32), "eye_f").ap()
    t2blk = np.zeros((128, 128), dtype=np.float32)
    t2blk[:64, :64] = np.triu(np.ones((64, 64), dtype=np.float32))
    t2blk[64:, 64:] = np.triu(np.ones((64, 64), dtype=np.float32))
    T2F = nc.inline_tensor(t2blk, "T2F").ap()
    bones = np.zeros((128, 128), dtype=np.float32)
    bones[:64, :64] = 1.0
    bones[64:, 64:] = 1.0
    BONES = nc.inline_tensor(bones, "BONES").ap()
    sel = np.zeros((4, 32, 48), dtype=np.float32)
    for i in range(16):
        sel[0, 16 + i, i] = 1.0
        sel[1, 16 + i, 32 + i] = 1.0
        sel[2, i, i] = 1.0
        sel[3, i, 32 + i] = 1.0
    SEL = nc.inline_tensor(bf(sel.transpose(1, 0, 2).reshape(32, 192)), "SEL").ap()

    es = ExitStack()
    with es:
        sb = lambda name, shape, dt: es.enter_context(nc.sbuf_tensor(name, shape, dt))
        ps = lambda name, shape, dt: es.enter_context(nc.psum_tensor(name, shape, dt))
        sem = lambda name: es.enter_context(nc.semaphore(name))

        Wsl = [sb(f"Wsl{i}", [128, 8 * EW], BF16) for i in range(2)]
        Wob = sb("Wob", [128, 2048], BF16)
        x_pk = sb("x_pk", [128, 4096], F32)
        xn_pk = sb("xn_pk", [128, 4096], BF16)
        scr_f = sb("scr_f", [128, 4096], F32)
        sz_pk = sb("sz_pk", [128, 8192], BF16)
        xh_pk = sb("xh_pk", [128, 8192], BF16)
        y2_pk = sb("y2_pk", [128, 8192], BF16)
        xbc = sb("xbc", [128, 17 * 518], BF16)
        BCa = sb("BCa", [32, 512], BF16)
        dt_sb = sb("dt_sb", [64, 512], F32)
        dtT_sb = sb("dtT_sb", [128, 64], F32)
        acs_sb = sb("acs_sb", [128, 32], F32)
        ds_tmp = sb("ds_tmp", [128, 32], F32)
        G_sb = sb("G_sb", [128, 32], F32)
        dtds = sb("dtds", [128, 32], BF16)
        dtb16 = sb("dtb16", [128, 32], BF16)
        M3 = sb("M3", [128, 128], F32)
        Mdec = sb("Mdec", [128, 4096], BF16)
        Btd = sb("Btd", [128, 48], BF16)
        BfCf = sb("BfCf", [48, 256], BF16)
        lam = [sb(f"lam{i}", [16, 32], BF16) for i in range(2)]
        lam0 = sb("lamz", [16, 32], BF16)
        Rp = [sb(f"Rp{i}", [48, 2048], BF16) for i in range(2)]
        Rz1c = sb("Rz1c", [16, 2048], BF16)
        Rtmp = sb("Rtmp", [16, 2048], BF16)
        xh_tok = sb("xh_tok", [128, 2048], BF16)
        xdd = sb("xdd", [128, 2048], BF16)
        xhD = sb("xhD", [128, 2048], BF16)
        rowb = sb("rowb", [1, 2048], F32)
        rms_b = sb("rms_b", [128, 512], BF16)
        cb_sb = sb("cb_sb", [128, 17], F32)
        cw_sb = sb("cw_sb", [128, 68], F32)
        dtb_sb = sb("dtb_sb", [64, 1], F32)
        acol = sb("acol", [64, 1], F32)
        Drep = sb("Drep", [128, 32], BF16)
        nw_sb = sb("nw_sb", [128, 16], F32)
        sw_sb = sb("sw_sb", [128, 8], F32)
        fb_sb = sb("fb_sb", [128, 8], F32)
        swx = sb("swx", [128, 8], F32)
        mx_acc = sb("mx_acc", [128, 1], F32)
        mxcols = sb("mxcols", [128, NSPAN], F32)
        cc_sb = sb("cc_sb", [1, 32], F32)
        srep = sb("srep", [128, 1], F32)
        sinv = sb("sinv", [128, 1], F32)
        srow1 = sb("srow1", [1, 1], F32)
        ones_r = sb("ones_r", [1, 128], F32)
        ones_cb = sb("ones_cb", [128, 1], BF16)
        eyeb_sb = sb("eyeb_sb", [128, 128], BF16)
        eyef_sb = sb("eyef_sb", [128, 128], F32)
        t2_sb = sb("t2_sb", [128, 128], F32)
        bo_sb = sb("bo_sb", [128, 128], F32)
        sel_sb = sb("sel_sb", [32, 192], BF16)

        PA = ps("PA", [128, 512], F32)
        PB = ps("PB", [128, 512], F32)
        PY = ps("PY", [128, 1024], F32)
        PYO = ps("PYO", [128, 1024], F32)
        PT = ps("PT", [128, 2048], BF16)

        sLD = sem("sLD"); sST = sem("sST"); sGD = sem("sGD")
        sT = sem("sT"); sV = sem("sV"); sSC = sem("sSC"); sCC = sem("sCC")
        n = dict(ld=0, st=0, gd=0, t=0, v=0, sc=0, cc=0)

        def iT(inst): n['t'] += 1; return inst.then_inc(sT, 1)
        def iV(inst): n['v'] += 1; return inst.then_inc(sV, 1)
        def iS(inst): n['sc'] += 1; return inst.then_inc(sSC, 1)
        def iLD(inst): n['ld'] += 16; return inst.then_inc(sLD, 16)
        def iST(inst): n['st'] += 16; return inst.then_inc(sST, 16)
        def iGD(inst): n['gd'] += 16; return inst.then_inc(sGD, 16)

        def wv(e): e.wait_ge(sV, n['v'])
        def wt(e): e.wait_ge(sT, n['t'])
        def wsc(e): e.wait_ge(sSC, n['sc'])

        def b3(t, off, d0, d1, d2):
            a = t[:, :]
            return AP(a.tensor, a.offset + off, [[a.ap[0][0], d0[1]] if d0[0] is None else d0,
                                                 d1, d2])

        blk = es.enter_context(nc.Block())

        @blk.sync
        def _(sy):
            iLD(sy.dma_start(out=cb_sb[:, :], in_=cbT))
            iLD(sy.dma_start(out=cw_sb[:, :], in_=cw))
            iLD(sy.dma_start(out=dtb_sb[32:64, :], in_=dtb))
            iLD(sy.dma_start(out=acol[32:64, :], in_=alog))
            iLD(sy.dma_start(out=nw_sb[:, :], in_=nw16))
            iLD(sy.dma_start(out=sw_sb[:, :], in_=sw8))
            iLD(sy.dma_start(out=fb_sb[:, :], in_=fb8))
            iLD(sy.dma_start(out=eyeb_sb[:, :], in_=eye_b))
            iLD(sy.dma_start(out=eyef_sb[:, :], in_=eye_f))
            iLD(sy.dma_start(out=t2_sb[:, :], in_=T2F))
            iLD(sy.dma_start(out=bo_sb[:, :], in_=BONES))
            iLD(sy.dma_start(out=sel_sb[:, :], in_=SEL))
            iLD(sy.dma_start(out=cc_sb[:, :], in_=dvec))
        ld_pro = n['ld']

        @blk.gpsimd
        def _(g):
            iGD(g.dma_start(out=out_fm, in_=x_fm))
        gd_xcopy = n['gd']

        @blk.vector
        def _(v):
            v.wait_ge(sLD, ld_pro)
            v.memset(ones_r[:, :], 1.0)
            v.memset(ones_cb[:, :], 1.0)
            v.memset(Btd[:, :], 0.0)
            v.memset(Rp[0][:, :], 0.0)
            v.memset(Rp[1][:, :], 0.0)
            v.memset(lam[0][:, :], 0.0)
            v.memset(lam[1][:, :], 0.0)
            v.memset(mx_acc[:, :], 0.0)
            iV(v.memset(xbc[:, :], 0.0))
        v0 = n['v']

        @blk.scalar
        def _(s_):
            s_.wait_ge(sV, v0)
            iS(s_.activation(acol[32:64, :], acol[32:64, :], ACT.Exp))
        sc0 = n['sc']

        @blk.vector
        def _(v):
            v.wait_ge(sSC, sc0)
            iV(v.tensor_single_scalar(acol[32:64, :], acol[32:64, :], -1.0, ALU.mult))
        v1 = n['v']

        @blk.tensor
        def _(t):
            t.wait_ge(sV, v1)
            iT(t.matmul(PA[:, 0:32], ones_r[:, :], cc_sb[:, :], start=True, stop=True))
        t0c = n['t']

        @blk.vector
        def _(v):
            v.wait_ge(sT, t0c)
            iV(v.tensor_copy(Drep[:, :], PA[:, 0:32]))
        v2 = n['v']

        for k in range(8):
            @blk.gpsimd
            def _(g, k=k):
                if k == 0:
                    g.wait_ge(sV, v2)
                if k >= 2:
                    g.wait_ge(sST, (k - 1) * 16)
                buf = xh_pk if (k % 2 == 0) else y2_pk
                iGD(g.dma_start(out=buf[:, 0:DIP], in_=w_inT[k * 128:(k + 1) * 128, :]))
            @blk.sync
            def _(sy, k=k):
                sy.wait_ge(sGD, gd_xcopy + (k + 1) * 16)
                buf = xh_pk if (k % 2 == 0) else y2_pk
                iST(sy.dma_start(out=WBF[k], in_=buf[:, 0:DIP]))
        st_wbf = n['st']

        for i in range(16):
            @blk.gpsimd
            def _(g, i=i):
                if i == 0:
                    g.wait_ge(sST, st_wbf)
                if i >= 2:
                    g.wait_ge(sST, st_wbf + (i - 1) * 16)
                iGD(g.dma_start(out=xh_pk[:, (i % 2) * 1024:(i % 2) * 1024 + 1024],
                                in_=w_outT[i * 128:(i + 1) * 128, :]))
            @blk.vector
            def _(v, i=i):
                v.wait_ge(sGD, gd_xcopy + 128 + (i + 1) * 16)
                h = (i % 2) * 1024
                iV(v.tensor_scalar(xh_pk[:, h:h + 1024], xh_pk[:, h:h + 1024],
                                   nw_sb[:, i:i + 1], None, ALU.mult))
            @blk.sync
            def _(sy, i=i):
                sy.wait_ge(sV, v2 + i + 1)
                h = (i % 2) * 1024
                a = xh_pk[:, h:h + 1024]
                dst3 = AP(WOB.tensor, i * 128, [[DI, 128], [128 * DI, 8], [1, 128]])
                src3 = AP(a.tensor, a.offset, [[a.ap[0][0], 128], [128, 8], [1, 128]])
                iST(sy.dma_start(out=dst3, in_=src3))
        st_wob = n['st']

        for s in range(NSPAN):
            @blk.sync
            def _(sy, s=s):
                if s == 0:
                    sy.wait_ge(sST, st_wob)
                    w_load0 = AP(WBF.tensor, 0, [[DIP, 128], [128 * DIP, 8], [1, 8 * 128]])
                    dst0 = b3(Wsl[0], 0, [None, 128], [EW, 8], [1, 8 * 128])
                    iLD(sy.dma_start(out=dst0, in_=w_load0))
                wv(sy)
                dst = b3(x_pk, 0, [None, 128], [512, 8], [1, 512])
                src = AP(x_fm.tensor, s * 512, [[L, 128], [128 * L, 8], [1, 512]])
                iLD(sy.dma_start(out=dst, in_=src))
            ld_x = n['ld']

            @blk.vector
            def _(v, s=s, ld_x=ld_x):
                v.wait_ge(sLD, ld_x)
                wsc(v); wt(v)
                src = b3(xbc, 512, [None, 128], [518, 17], [1, 3])
                dst = b3(xbc, 0, [None, 128], [518, 17], [1, 3])
                iV(v.tensor_copy(dst, src))
                iV(v.tensor_copy(xh_pk[:, 4096:8192], x_pk[:, :]))
            vb = n['v']

            @blk.scalar
            def _(s_, vb=vb):
                s_.wait_ge(sV, vb)
                iS(s_.activation(xh_pk[:, 0:4096], xh_pk[:, 4096:8192], ACT.Square))
            scb = n['sc']

            @blk.tensor
            def _(t, scb=scb):
                t.wait_ge(sSC, scb)
                for k in range(8):
                    iT(t.matmul(PA[0:1, 0:512], ones_cb[:, :], xh_pk[:, 4096 + k * 512:4096 + (k + 1) * 512],
                                start=(k == 0), stop=(k == 7)))
                for k in range(8):
                    iT(t.matmul(PB[0:1, 0:512], ones_cb[:, :], xh_pk[:, k * 512:(k + 1) * 512],
                                start=(k == 0), stop=(k == 7)))
            tb = n['t']

            @blk.vector
            def _(v, tb=tb):
                v.wait_ge(sT, tb)
                iV(v.tensor_scalar(rowb[:, 0:512], PA[0:1, 0:512], 1.0 / DM, None, ALU.mult))
                iV(v.tensor_scalar(rowb[:, 512:1024], PB[0:1, 0:512], 1.0 / DM, 1e-5, ALU.mult, ALU.add))
                iV(v.tensor_tensor(rowb[:, 1536:2048], rowb[:, 0:512], rowb[:, 0:512], ALU.mult))
                iV(v.scalar_tensor_tensor(rowb[:, 1536:2048], rowb[:, 1536:2048], -1.0,
                                          rowb[:, 512:1024], ALU.mult, ALU.add))
                iV(v.reciprocal(rowb[:, 512:1024], rowb[:, 1536:2048]))
            vr1 = n['v']

            @blk.scalar
            def _(s_, vr1=vr1):
                s_.wait_ge(sV, vr1)
                iS(s_.activation(rowb[:, 1024:1536], rowb[:, 512:1024], ACT.Sqrt))
            scr = n['sc']

            @blk.vector
            def _(v, scr=scr):
                v.wait_ge(sSC, scr)
                iV(v.tensor_tensor(rowb[:, 1536:2048], rowb[:, 0:512], rowb[:, 1024:1536], ALU.mult))
            vr2 = n['v']

            @blk.tensor
            def _(t, vr2=vr2):
                t.wait_ge(sV, vr2)
                iT(t.matmul(PY[:, 0:512], ones_r[:, :], rowb[0:1, 1024:1536], start=True, stop=True))
                iT(t.matmul(PYO[:, 0:512], ones_r[:, :], rowb[0:1, 1536:2048], start=True, stop=True))
            tbc = n['t']

            @blk.vector
            def _(v, tbc=tbc, s=s):
                v.wait_ge(sT, tbc)
                a_ap = AP(PY[:, :].tensor, PY[:, :].offset, [[PY[:, :].ap[0][0], 128], [0, 8], [1, 512]])
                b_ap = AP(PYO[:, :].tensor, PYO[:, :].offset, [[PYO[:, :].ap[0][0], 128], [0, 8], [1, 512]])
                iV(v.tensor_tensor(scr_f[:, :], x_pk[:, :], a_ap, ALU.mult))
                iV(v.tensor_tensor(xn_pk[:, :], scr_f[:, :], b_ap, ALU.subtract))
            vxn = n['v']

            @blk.scalar
            def _(s_, vxn=vxn):
                s_.wait_ge(sV, vxn)
                iS(s_.activation(xh_pk[:, 0:4096], xn_pk[:, :], ACT.Abs))
            sc_abs = n['sc']
            @blk.vector
            def _(v, sc_abs=sc_abs, s=s):
                v.wait_ge(sSC, sc_abs)
                iV(v.tensor_reduce(mxcols[:, s:s + 1], xh_pk[:, 0:4096], AXL.X, ALU.max))

            @blk.sync
            def _(sy, s=s, vxn=vxn):
                sy.wait_ge(sV, vxn)
                iST(sy.dma_start(out=XN[s], in_=xn_pk[:, :]))
            if DBG and s == 0:
                @blk.gpsimd
                def _(g, vxn=vxn):
                    g.wait_ge(sV, vxn)
                    iGD(g.dma_start(out=dbg["dbg_xn"], in_=xn_pk[:, :]))


            evac_cnt = {}
            for esl in range(4):
                e0, e1 = ESLICES[esl]
                @blk.sync
                def _(sy, s=s, esl=esl):
                    sy.wait_ge(sT, n['t'])
                    nsl = esl + 1
                    if nsl < 4:
                        cols = min(ESLICES[nsl][1] * 128, DIP) - ESLICES[nsl][0] * 128
                        dst = b3(Wsl[nsl % 2], 0, [None, 128], [EW, 8], [1, cols])
                        src = AP(WBF.tensor, ESLICES[nsl][0] * 128,
                                 [[DIP, 128], [128 * DIP, 8], [1, cols]])
                        iLD(sy.dma_start(out=dst, in_=src))
                    elif s + 1 < NSPAN:
                        dst = b3(Wsl[0], 0, [None, 128], [EW, 8], [1, 8 * 128])
                        src = AP(WBF.tensor, 0, [[DIP, 128], [128 * DIP, 8], [1, 8 * 128]])
                        iLD(sy.dma_start(out=dst, in_=src))
                wneed = n['ld']

                @blk.tensor
                def _(t, esl=esl, e0=e0, e1=e1, wneed=wneed, vb=vb, vxn=vxn):
                    t.wait_ge(sLD, wneed - 16)
                    if esl == 0:
                        t.wait_ge(sV, max(vb, vxn))
                    for e in range(e0, e1):
                        pp = PA if (e % 2 == 0) else PB
                        if e - 2 >= 0:
                            t.wait_ge(sSC, evac_cnt[e - 2])
                        mrows = 128 if e < 32 else 64
                        for k in range(8):
                            iT(t.matmul(pp[0:mrows, 0:512],
                                        Wsl[esl % 2][:, k * EW + (e - e0) * 128:k * EW + (e - e0) * 128 + mrows],
                                        xn_pk[:, k * 512:(k + 1) * 512],
                                        start=(k == 0), stop=(k == 7)))
                        tcnt = n['t']
                        if e < 16:
                            @blk.scalar
                            def _(s_, e=e, pp=pp, tcnt=tcnt):
                                s_.wait_ge(sT, tcnt)
                                iS(s_.activation(sz_pk[:, e * 512:(e + 1) * 512], pp[:, 0:512], ACT.Silu))
                        elif e < 32:
                            g = e - 16
                            @blk.scalar
                            def _(s_, g=g, pp=pp, tcnt=tcnt):
                                s_.wait_ge(sT, tcnt)
                                iS(s_.activation(xbc[:, g * 518 + 3:g * 518 + 515], pp[:, 0:512], ACT.Copy))
                        else:
                            @blk.scalar
                            def _(s_, pp=pp, tcnt=tcnt):
                                s_.wait_ge(sT, tcnt)
                                iS(s_.activation(xbc[0:32, 16 * 518 + 3:16 * 518 + 515], pp[0:32, 0:512], ACT.Copy))
                        evac_cnt[e] = n['sc']
            sc_ip = n['sc']

            @blk.vector
            def _(v, sc_ip=sc_ip):
                v.wait_ge(sSC, sc_ip)
                iV(v.tensor_scalar(dt_sb[32:64, :], PA[32:64, 0:512], dtb_sb[32:64, :], None, ALU.add))
            v_dtb = n['v']

            @blk.scalar
            def _(s_, sc_ip=sc_ip, v_dtb=v_dtb):
                s_.wait_ge(sV, v_dtb)
                iS(s_.activation(scr_f[32:64, 0:512], dt_sb[32:64, :], ACT.Abs))
            @blk.scalar
            def _(s_):
                iS(s_.activation(scr_f[32:64, 512:1024], scr_f[32:64, 0:512], ACT.Exp, bias=0.0, scale=-1.0))
            @blk.scalar
            def _(s_):
                iS(s_.activation(scr_f[32:64, 512:1024], scr_f[32:64, 512:1024], ACT.Ln, bias=1.0, scale=1.0))
            sc_dt = n['sc']
            @blk.vector
            def _(v, sc_dt=sc_dt):
                v.wait_ge(sSC, sc_dt)
                iV(v.tensor_tensor(scr_f[32:64, 0:512], dt_sb[32:64, :], scr_f[32:64, 0:512], ALU.add))
                iV(v.scalar_tensor_tensor(dt_sb[32:64, :], scr_f[32:64, 0:512], 0.5,
                                          scr_f[32:64, 512:1024], ALU.mult, ALU.add))
            v_dt = n['v']

            if DBG and s == 0:
                @blk.gpsimd
                def _(g, v_dt=v_dt):
                    g.wait_ge(sV, v_dt)
                    wsc(g)
                    iGD(g.dma_start(out=dbg["dbg_sz"], in_=sz_pk[:, :]))
                    iGD(g.dma_start(out=dbg["dbg_dt"], in_=dt_sb[32:64, :]))

            for g in range(17):
                rr = 128 if g < 16 else 32
                dst = (xh_pk[:, g * 512:(g + 1) * 512] if g < 16 else BCa[:, :])
                @blk.vector
                def _(v, g=g, rr=rr, dst=dst, sc_ip=sc_ip):
                    if g == 0:
                        v.wait_ge(sSC, sc_ip)
                    d = dst if rr == 128 else dst
                    iV(v.tensor_scalar(d, xbc[0:rr, g * 518 + 0:g * 518 + 512],
                                       cw_sb[0:rr, g * 4:g * 4 + 1], None, ALU.mult))
                    for k in (1, 2, 3):
                        iV(v.scalar_tensor_tensor(d, xbc[0:rr, g * 518 + k:g * 518 + k + 512],
                                                  cw_sb[0:rr, g * 4 + k:g * 4 + k + 1], d,
                                                  ALU.mult, ALU.add))
                vcv = n['v']
                @blk.scalar
                def _(s_, g=g, rr=rr, dst=dst, vcv=vcv):
                    s_.wait_ge(sV, vcv)
                    iS(s_.activation(dst, dst, ACT.Silu, bias=cb_sb[0:rr, g:g + 1], scale=1.0))
            sc_cv = n['sc']

            if DBG and s == 0:
                @blk.gpsimd
                def _(g, sc_cv=sc_cv):
                    g.wait_ge(sSC, sc_cv)
                    iGD(g.dma_start(out=dbg["dbg_xh"], in_=xh_pk[:, :]))
                    iGD(g.dma_start(out=dbg["dbg_bca"], in_=BCa[:, :]))

            for p in range(PPS):
                gp = s * PPS + p
                pc = p * 128
                @blk.tensor
                def _(t, p=p, pc=pc, sc_cv=sc_cv):
                    wsc(t); wv(t)
                    iT(t.transpose(PT[:, 0:16], BCa[0:16, pc:pc + 128], eyeb_sb[0:16, 0:16]))
                    iT(t.matmul(PB[0:48, 256:320], sel_sb[:, 0:48], BCa[0:32, pc:pc + 64], start=True, stop=True))
                    iT(t.matmul(PB[0:48, 320:384], sel_sb[:, 48:96], BCa[0:32, pc + 64:pc + 128], start=True, stop=True))
                    iT(t.matmul(PB[0:48, 384:448], sel_sb[:, 96:144], BCa[0:32, pc:pc + 64], start=True, stop=True))
                    iT(t.matmul(PB[0:48, 448:512], sel_sb[:, 144:192], BCa[0:32, pc + 64:pc + 128], start=True, stop=True))
                    iT(t.transpose(PB[:, 0:32], dt_sb[32:64, pc:pc + 128], eyef_sb[32:64, 32:64]))
                tc1a = n['t']

                @blk.vector
                def _(v, pc=pc, tc1a=tc1a):
                    v.wait_ge(sT, tc1a)
                    iV(v.tensor_scalar(scr_f[32:64, pc:pc + 128],
                                       dt_sb[32:64, pc:pc + 128], acol[32:64, :], None, ALU.mult))
                    iV(v.tensor_copy(BfCf[:, 0:128], PB[0:48, 256:384]))
                    iV(v.tensor_copy(BfCf[:, 128:256], PB[0:48, 384:512]))
                    iV(v.tensor_copy(Btd[0:64, 0:16], PT[0:64, 0:16]))
                    iV(v.tensor_copy(Btd[64:128, 32:48], PT[64:128, 0:16]))
                    iV(v.tensor_copy(dtT_sb[:, 0:32], PB[:, 0:32]))
                vc1 = n['v']

                @blk.tensor
                def _(t, pc=pc, vc1=vc1):
                    t.wait_ge(sV, vc1)
                    iT(t.transpose(PB[:, 32:64], scr_f[32:64, pc:pc + 128], eyef_sb[32:64, 32:64]))
                tc1b = n['t']
                @blk.vector
                def _(v, tc1b=tc1b):
                    v.wait_ge(sT, tc1b)
                    iV(v.tensor_copy(dtT_sb[:, 32:64], PB[:, 32:64]))
                vc1b = n['v']

                @blk.tensor
                def _(t, vc1b=vc1b):
                    t.wait_ge(sV, vc1b)
                    iT(t.matmul(PB[:, 64:96], t2_sb[:, :], dtT_sb[:, 32:64], start=True, stop=True))
                    iT(t.matmul(PB[:, 96:128], bo_sb[:, :], dtT_sb[:, 32:64], start=True, stop=True))
                    iT(t.matmul(PB[:, 128:256], BfCf[:, 128:256], BfCf[:, 0:128], start=True, stop=True))
                tc2 = n['t']

                @blk.vector
                def _(v, tc2=tc2):
                    v.wait_ge(sT, tc2)
                    iV(v.tensor_copy(acs_sb[:, :], PB[:, 64:96]))
                vc2 = n['v']

                @blk.scalar
                def _(s_, vc2=vc2):
                    s_.wait_ge(sV, vc2)
                    iS(s_.activation(G_sb[:, :], PB[:, 64:96], ACT.Exp))
                    iS(s_.activation(lam0[:, :], PB[0:16, 96:128], ACT.Exp))
                sc_g = n['sc']

                @blk.vector
                def _(v, sc_g=sc_g):
                    v.wait_ge(sSC, sc_g)
                    iV(v.tensor_tensor(ds_tmp[:, :], PB[:, 96:128], acs_sb[:, :], ALU.subtract))
                vds = n['v']

                @blk.scalar
                def _(s_, vds=vds, gp=gp):
                    s_.wait_ge(sV, vds)
                    iS(s_.activation(dtds[:, :], ds_tmp[:, :], ACT.Exp))
                    iS(s_.activation(lam[gp % 2][:, :], PB[64:80, 96:128], ACT.Exp))
                sc_ds = n['sc']

                @blk.vector
                def _(v, sc_ds=sc_ds):
                    v.wait_ge(sSC, sc_ds)
                    iV(v.tensor_copy(dtb16[:, :], dtT_sb[:, 0:32]))
                    iV(v.tensor_tensor(M3[:, :], PB[:, 128:256], t2_sb[:, :], ALU.mult))
                vm3 = n['v']

                @blk.sync
                def _(sy, gp=gp, vc2=vc2):
                    sy.wait_ge(sV, vc2)
                    iST(sy.dma_start(out=ACSB[gp % 2], in_=acs_sb[:, :]))
                st_acs = n['st']

                @blk.tensor
                def _(t, p=p, vc1=vc1):
                    t.wait_ge(sV, vc1)
                    for k in range(16):
                        iT(t.transpose(PT[:, k * 128:(k + 1) * 128],
                                       xh_pk[:, k * 512 + p * 128:k * 512 + p * 128 + 128],
                                       eyeb_sb[:, :]))
                t_xh = n['t']
                @blk.vector
                def _(v, t_xh=t_xh):
                    v.wait_ge(sT, t_xh)
                    iV(v.tensor_copy(xh_tok[:, :], PT[:, :]))
                v_xh = n['v']

                @blk.vector
                def _(v, v_xh=v_xh, vm3=vm3):
                    v.wait_ge(sV, vm3)
                    dd = dtds[:, :]
                    ds_ap = AP(dd.tensor, dd.offset, [[dd.ap[0][0], 128], [1, 32], [0, 64]])
                    db = dtb16[:, :]
                    dt_ap = AP(db.tensor, db.offset, [[db.ap[0][0], 128], [1, 32], [0, 64]])
                    dr = Drep[:, :]
                    drep_ap = AP(dr.tensor, dr.offset, [[dr.ap[0][0], 128], [1, 32], [0, 64]])
                    iV(v.tensor_tensor(xhD[:, :], xh_tok[:, :], drep_ap, ALU.mult))
                    iV(v.tensor_tensor(xh_tok[:, :], xh_tok[:, :], dt_ap, ALU.mult))
                    iV(v.tensor_tensor(xdd[:, :], xh_tok[:, :], ds_ap, ALU.mult))
                v_xdd = n['v']

                for hq in range(2):
                    @blk.sync
                    def _(sy, gp=gp, hq=hq, st_acs=st_acs):
                        sy.wait_ge(sST, st_acs)
                        wv(sy)
                        src = AP(ACSB.tensor, (gp % 2) * 4096 + hq * 2048, [[2048, 1], [1, 2048]])
                        iLD(sy.dma_start(out=rowb[:, :], in_=src))
                    ld_row = n['ld']
                    for qq in range(2):
                        q = hq * 2 + qq
                        pq = PY if (qq == 0) else PYO
                        @blk.tensor
                        def _(t, q=q, qq=qq, pq=pq, ld_row=ld_row, vm3=vm3, v_xdd=v_xdd):
                            t.wait_ge(sLD, ld_row)
                            t.wait_ge(sV, max(vm3, v_xdd))
                            wv(t)
                            iT(t.matmul(pq[:, 0:512], ones_r[:, :], rowb[0:1, qq * 1024:qq * 1024 + 512],
                                        start=True, stop=True))
                            iT(t.matmul(pq[:, 512:1024], ones_r[:, :], rowb[0:1, qq * 1024 + 512:(qq + 1) * 1024],
                                        start=True, stop=True))
                        t_b2 = n['t']
                        @blk.vector
                        def _(v, pq=pq, t_b2=t_b2):
                            v.wait_ge(sT, t_b2)
                            aa = acs_sb[:, :]
                            acs_l0 = AP(aa.tensor, aa.offset, [[aa.ap[0][0], 128], [0, 32], [1, 32]])
                            iV(v.scalar_tensor_tensor(scr_f[:, 0:1024], acs_l0, -1.0, pq[:, :],
                                                      ALU.mult, ALU.add))
                            iV(v.tensor_single_scalar(scr_f[:, 0:1024], scr_f[:, 0:1024], 0.0, ALU.min))
                        v_d = n['v']
                        @blk.scalar
                        def _(s_, v_d=v_d):
                            s_.wait_ge(sV, v_d)
                            iS(s_.activation(scr_f[:, 1024:2048], scr_f[:, 0:1024], ACT.Exp))
                        sc_e = n['sc']
                        @blk.vector
                        def _(v, q=q, sc_e=sc_e):
                            v.wait_ge(sSC, sc_e)
                            mm = M3[:, :]
                            m3_ap = AP(mm.tensor, mm.offset + q * 32, [[mm.ap[0][0], 128], [1, 32], [0, 32]])
                            iV(v.tensor_tensor(Mdec[:, q * 1024:(q + 1) * 1024], scr_f[:, 1024:2048],
                                               m3_ap, ALU.mult))
                v_mdec = n['v']

                if DBG and gp == 0:
                    @blk.gpsimd
                    def _(g, v_mdec=v_mdec):
                        g.wait_ge(sV, v_mdec)
                        iGD(g.dma_start(out=dbg["dbg_mdec"], in_=Mdec[:, :]))

                @blk.tensor
                def _(t, v_mdec=v_mdec, v_xdd=v_xdd):
                    t.wait_ge(sV, max(v_mdec, v_xdd))
                    iT(t.matmul(PY[0:48, 0:512], Btd[:, :], xdd[:, 0:512], start=True, stop=True))
                    iT(t.matmul(PY[0:48, 512:1024], Btd[:, :], xdd[:, 512:1024], start=True, stop=True))
                    iT(t.matmul(PYO[0:48, 0:512], Btd[:, :], xdd[:, 1024:1536], start=True, stop=True))
                    iT(t.matmul(PYO[0:48, 512:1024], Btd[:, :], xdd[:, 1536:2048], start=True, stop=True))
                t_st = n['t']
                @blk.vector
                def _(v, gp=gp, t_st=t_st):
                    v.wait_ge(sT, t_st)
                    l0 = lam0[:, :]
                    lam0_ap = AP(l0.tensor, l0.offset, [[l0.ap[0][0], 16], [1, 32], [0, 64]])
                    iV(v.tensor_tensor(Rtmp[:, :], Rp[gp % 2][0:16, :], lam0_ap, ALU.mult))
                    iV(v.tensor_tensor(Rz1c[:, 0:1024], Rtmp[:, 0:1024], PY[0:16, :], ALU.add))
                    iV(v.tensor_tensor(Rz1c[:, 1024:2048], Rtmp[:, 1024:2048], PYO[0:16, :], ALU.add))
                    iV(v.tensor_copy(Rp[gp % 2][32:48, :], Rz1c[:, :]))
                    lp = lam[gp % 2][:, :]
                    lam_ap = AP(lp.tensor, lp.offset, [[lp.ap[0][0], 16], [1, 32], [0, 64]])
                    iV(v.tensor_tensor(Rtmp[:, :], Rz1c[:, :], lam_ap, ALU.mult))
                    iV(v.tensor_tensor(Rp[(gp + 1) % 2][0:16, 0:1024], Rtmp[:, 0:1024], PY[32:48, :], ALU.add))
                    iV(v.tensor_tensor(Rp[(gp + 1) % 2][0:16, 1024:2048], Rtmp[:, 1024:2048], PYO[32:48, :], ALU.add))
                v_r = n['v']

                for hh in range(2):
                    @blk.tensor
                    def _(t, hh=hh, gp=gp, v_r=v_r):
                        t.wait_ge(sV, n['v'])
                        for h2 in range(16):
                            h = hh * 16 + h2
                            md = Mdec[:, :]
                            md_ap = AP(md.tensor, md.offset + h, [[md.ap[0][0], 128], [32, 128]])
                            iT(t.matmul(PY[:, h2 * 64:(h2 + 1) * 64], md_ap,
                                        xh_tok[:, h * 64:(h + 1) * 64],
                                        start=(h2 % 8 == 0), stop=False))
                        iT(t.matmul(PY[:, 0:512], eyeb_sb[:, :], xhD[:, hh * 1024:hh * 1024 + 512],
                                    start=False, stop=True))
                        iT(t.matmul(PY[:, 512:1024], eyeb_sb[:, :], xhD[:, hh * 1024 + 512:(hh + 1) * 1024],
                                    start=False, stop=True))
                        iT(t.matmul(PYO[:, 0:512], BfCf[:, 0:128], Rp[gp % 2][:, hh * 1024:hh * 1024 + 512],
                                    start=True, stop=True))
                        iT(t.matmul(PYO[:, 512:1024], BfCf[:, 0:128],
                                    Rp[gp % 2][:, hh * 1024 + 512:(hh + 1) * 1024], start=True, stop=True))
                    t_y = n['t']
                    @blk.vector
                    def _(v, hh=hh, t_y=t_y):
                        v.wait_ge(sT, t_y)
                        gg = G_sb[:, :]
                        g_ap = AP(gg.tensor, gg.offset + hh * 16, [[gg.ap[0][0], 128], [1, 16], [0, 64]])
                        iV(v.tensor_tensor(scr_f[:, 1024:2048], PYO[:, :], g_ap, ALU.mult))
                        iV(v.tensor_tensor(xdd[:, hh * 1024:(hh + 1) * 1024], scr_f[:, 1024:2048],
                                           PY[:, :], ALU.add))
                v_y1 = n['v']

                @blk.tensor
                def _(t, v_y1=v_y1):
                    t.wait_ge(sV, v_y1)
                    for k in range(16):
                        iT(t.transpose(PT[:, k * 128:(k + 1) * 128],
                                       xdd[:, k * 128:(k + 1) * 128], eyeb_sb[:, :]))
                t_yt = n['t']
                @blk.vector
                def _(v, p=p, t_yt=t_yt):
                    v.wait_ge(sT, t_yt)
                    sz = sz_pk[:, :]
                    sz_ap = AP(sz.tensor, sz.offset + p * 128, [[sz.ap[0][0], 128], [512, 16], [1, 128]])
                    yy = y2_pk[:, :]
                    y2_ap = AP(yy.tensor, yy.offset + p * 128, [[yy.ap[0][0], 128], [512, 16], [1, 128]])
                    iV(v.tensor_tensor(y2_ap, PT[:, :], sz_ap, ALU.mult))

            if DBG and s == 0:
                @blk.gpsimd
                def _(g):
                    wv(g)
                    iGD(g.dma_start(out=dbg["dbg_y2"], in_=y2_pk[:, :]))
            dbg_gd = n['gd']
            @blk.scalar
            def _(s_):
                wv(s_); wt(s_)
                if DBG:
                    s_.wait_ge(sGD, dbg_gd)
                iS(s_.activation(xh_pk[:, 0:4096], y2_pk[:, 0:4096], ACT.Square))
                iS(s_.activation(xh_pk[:, 4096:8192], y2_pk[:, 4096:8192], ACT.Square))
            sc_sq = n['sc']
            @blk.tensor
            def _(t, sc_sq=sc_sq):
                t.wait_ge(sSC, sc_sq)
                for k in range(16):
                    iT(t.matmul(PA[0:1, 0:512], ones_cb[:, :], xh_pk[:, k * 512:(k + 1) * 512],
                                start=(k == 0), stop=(k == 15)))
            t_rms = n['t']
            @blk.vector
            def _(v, t_rms=t_rms):
                v.wait_ge(sT, t_rms)
                iV(v.tensor_scalar(rowb[:, 0:512], PA[0:1, 0:512], 1.0 / DI, 1e-5, ALU.mult, ALU.add))
                iV(v.reciprocal(rowb[:, 512:1024], rowb[:, 0:512]))
            v_rms1 = n['v']
            @blk.scalar
            def _(s_, v_rms1=v_rms1):
                s_.wait_ge(sV, v_rms1)
                iS(s_.activation(rowb[:, 1024:1536], rowb[:, 512:1024], ACT.Sqrt))
            sc_rms = n['sc']
            @blk.tensor
            def _(t, sc_rms=sc_rms):
                t.wait_ge(sSC, sc_rms)
                iT(t.matmul(PB[:, 0:512], ones_r[:, :], rowb[0:1, 1024:1536], start=True, stop=True))
            t_rb = n['t']
            @blk.vector
            def _(v, t_rb=t_rb):
                v.wait_ge(sT, t_rb)
                iV(v.tensor_copy(rms_b[:, :], PB[:, 0:512]))
                rr = rms_b[:, :]
                rms_ap = AP(rr.tensor, rr.offset, [[rr.ap[0][0], 128], [0, 16], [1, 512]])
                iV(v.tensor_tensor(y2_pk[:, :], y2_pk[:, :], rms_ap, ALU.mult))
            v_y3 = n['v']

            ev7 = {}
            for d in range(8):
                @blk.sync
                def _(sy, d=d, s=s):
                    sy.wait_ge(sT, n['t'])
                    src = AP(WOB.tensor, d * 128 * DI, [[DI, 128], [1, 2048]])
                    iLD(sy.dma_start(out=Wob[:, :], in_=src))
                ld_wo = n['ld']
                pp = PA if (d % 2 == 0) else PB
                @blk.tensor
                def _(t, d=d, pp=pp, v_y3=v_y3, ld_wo=ld_wo):
                    t.wait_ge(sV, v_y3)
                    t.wait_ge(sLD, ld_wo)
                    if d >= 2:
                        t.wait_ge(sV, ev7[d - 2])
                    for i in range(16):
                        iT(t.matmul(pp[:, 0:512], Wob[:, i * 128:(i + 1) * 128],
                                    y2_pk[:, i * 512:(i + 1) * 512],
                                    start=(i == 0), stop=(i == 15)))
                t_op = n['t']
                @blk.vector
                def _(v, d=d, pp=pp, t_op=t_op):
                    v.wait_ge(sT, t_op)
                    iV(v.tensor_copy(scr_f[:, d * 512:(d + 1) * 512], pp[:, 0:512]))
                ev7[d] = n['v']
            v_att = n['v']

            if DBG and s == 0:
                @blk.sync
                def _(sy, v_att=v_att):
                    sy.wait_ge(sV, v_att)
                    iST(sy.dma_start(out=dbg["dbg_attn"], in_=scr_f[:, :]))
            @blk.gpsimd
            def _(g, s=s, v_att=v_att):
                g.wait_ge(sV, v_att)
                if s == 0:
                    g.wait_ge(sGD, gd_xcopy)
                    if DBG:
                        g.wait_ge(sST, n['st'])
                dst = AP(out_fm.tensor, s * 512, [[L, 128], [128 * L, 8], [1, 512]])
                a = scr_f[:, :]
                src = AP(a.tensor, a.offset, [[a.ap[0][0], 128], [512, 8], [1, 512]])
                iGD(g.dma_start(out=dst, in_=src, accum_op=ALU.add))
        gd_p1 = n['gd']
        v_p1 = n['v']; t_p1 = n['t']; sc_p1 = n['sc']; st_xn_all = n['st']

        @blk.gpsimd
        def _(g):
            for k in range(8):
                iGD(g.dma_start(out=Wsl[0][:, k * EW:k * EW + 1024], in_=wqT[k * 128:(k + 1) * 128, :]))
        gd_wq = n['gd']

        @blk.vector
        def _(v):
            iV(v.tensor_reduce(mx_acc[:, 0:1], mxcols[:, 0:NSPAN], AXL.X, ALU.max))
        @blk.tensor
        def _(t):
            wv(t); wsc(t)
            iT(t.transpose(PB[0:1, 0:128], mx_acc[:, :], eyef_sb[:, :]))
        t_mx = n['t']
        @blk.vector
        def _(v, t_mx=t_mx):
            v.wait_ge(sT, t_mx)
            v.memset(cc_sb[:, :], 0.0)
            iV(v.tensor_reduce(cc_sb[0:1, 0:1], PB[0:1, 0:128], AXL.X, ALU.max))
        v_mx = n['v']
        @blk.gpsimd
        def _(g, v_mx=v_mx):
            g.wait_ge(sV, v_mx)
            iGD(g.dma_start(out=CCIN, in_=cc_sb[:, :]))
            g.wait_ge(sGD, n['gd'])
            n['cc'] += 1
            g.collective_compute("AllReduce", ALU.max, replica_groups=RG,
                                 ins=[CCIN.opt()], outs=[CCOUT.opt()]).then_inc(sCC, 1)
            g.wait_ge(sCC, n['cc'])
            iGD(g.dma_start(out=cc_sb[:, :], in_=CCOUT))
        gd_cc = n['gd']
        @blk.vector
        def _(v, gd_cc=gd_cc):
            v.wait_ge(sGD, gd_cc)
            iV(v.tensor_scalar(srow1[:, :], cc_sb[0:1, 0:1], float(np.float32(1.0) / np.float32(127.0)),
                               1e-5, ALU.mult, ALU.max))
        v_s = n['v']
        @blk.tensor
        def _(t, v_s=v_s):
            t.wait_ge(sV, v_s)
            iT(t.matmul(PB[:, 0:1], ones_r[:, :], srow1[:, :], start=True, stop=True))
        t_s = n['t']
        @blk.vector
        def _(v, t_s=t_s):
            v.wait_ge(sT, t_s)
            iV(v.tensor_copy(srep[:, :], PB[:, 0:1]))
            iV(v.tensor_scalar(swx[:, :], sw_sb[:, :], PB[:, 0:1], None, ALU.mult))
            iV(v.reciprocal(sinv[:, :], PB[:, 0:1]))
        v_srep = n['v']

        if DBG:
            @blk.sync
            def _(sy, v_mx=v_mx, v_srep=v_srep):
                sy.wait_ge(sV, v_mx)
                iST(sy.dma_start(out=dbg["dbg_mx"], in_=mx_acc[:, :]))
                iST(sy.dma_start(out=dbg["dbg_mxcols"], in_=mxcols[:, :]))
                sy.wait_ge(sV, v_srep)
                iST(sy.dma_start(out=dbg["dbg_cc"], in_=cc_sb[:, :]))
                iST(sy.dma_start(out=dbg["dbg_srep"], in_=srep[:, :]))
                iST(sy.dma_start(out=dbg["dbg_sinv"], in_=sinv[:, :]))
                iST(sy.dma_start(out=dbg["dbg_swx"], in_=swx[:, :]))

        v_q_hist = {}
        gd_fstore = {}
        t_ffn = {}
        for s2 in range(NSPAN):
            qsl = (xn_pk[:, :] if s2 % 2 == 0 else sz_pk[:, 0:4096])
            @blk.sync
            def _(sy, s2=s2, qsl=qsl):
                if s2 <= 1:
                    sy.wait_ge(sV, v_p1); sy.wait_ge(sT, t_p1)
                    sy.wait_ge(sSC, sc_p1); sy.wait_ge(sST, st_xn_all)
                else:
                    sy.wait_ge(sV, v_q_hist[s2 - 2])
                iLD(sy.dma_start(out=qsl, in_=XN[s2]))
            ld_f = n['ld']
            qoff = (s2 % 2) * 4096
            @blk.vector
            def _(v, ld_f=ld_f, s2=s2, qsl=qsl, qoff=qoff):
                v.wait_ge(sLD, ld_f)
                if s2 == 0:
                    v.wait_ge(sGD, gd_wq)
                if s2 >= 1:
                    v.wait_ge(sGD, gd_fstore[s2 - 1])
                if s2 < 2:
                    wt(v)
                else:
                    v.wait_ge(sT, t_ffn[s2 - 2])
                iV(v.tensor_scalar(scr_f[:, :], qsl, sinv[:, :], 8388608.0,
                                   ALU.mult, ALU.add))
                iV(v.tensor_single_scalar(y2_pk[:, qoff:qoff + 4096], scr_f[:, :],
                                          8388608.0, ALU.subtract))
            v_q = n['v']
            v_q_hist[s2] = v_q
            if DBG and s2 == 0:
                @blk.gpsimd
                def _(g, v_q=v_q, ld_f=ld_f):
                    g.wait_ge(sV, v_q)
                    g.wait_ge(sLD, ld_f)
                    iGD(g.dma_start(out=dbg["dbg_q"], in_=y2_pk[:, 0:4096]))
            ffn_ev = {}
            for nn in range(8):
                pp = PA if (nn % 2 == 0) else PB
                @blk.tensor
                def _(t, nn=nn, pp=pp, v_q=v_q, qoff=qoff):
                    t.wait_ge(sV, v_q)
                    if nn >= 2:
                        t.wait_ge(sV, ffn_ev[nn - 2])
                    for k in range(8):
                        iT(t.matmul(pp[:, 0:512], Wsl[0][:, k * EW + nn * 128:k * EW + nn * 128 + 128],
                                    y2_pk[:, qoff + k * 512:qoff + (k + 1) * 512],
                                    start=(k == 0), stop=(k == 7)))
                t_f = n['t']
                @blk.vector
                def _(v, nn=nn, pp=pp, t_f=t_f):
                    v.wait_ge(sT, t_f)
                    iV(v.tensor_scalar(scr_f[:, nn * 512:(nn + 1) * 512], pp[:, 0:512],
                                       swx[:, nn:nn + 1], fb_sb[:, nn:nn + 1], ALU.mult, ALU.add))
                ffn_ev[nn] = n['v']
            t_ffn[s2] = n['t']
            @blk.gpsimd
            def _(g, s2=s2, last_ev=ffn_ev[7]):
                g.wait_ge(sV, last_ev)
                dst = AP(out_fm.tensor, s2 * 512, [[L, 128], [128 * L, 8], [1, 512]])
                a = scr_f[:, :]
                src = AP(a.tensor, a.offset, [[a.ap[0][0], 128], [512, 8], [1, 512]])
                iGD(g.dma_start(out=dst, in_=src, accum_op=ALU.add))
            gd_fstore[s2] = n['gd']

        @blk.sync
        def _(sy):
            sy.wait_ge(sST, n['st'])
            sy.wait_ge(sGD, n['gd'])

    return nc


_NC_CACHE = [None]


def kernel(L0_attn_x, L0_attn_W_in, L0_attn_conv_w, L0_attn_conv_b, L0_attn_dt_bias,
           L0_attn_A_log, L0_attn_D, L0_attn_norm_w, L0_attn_W_out,
           L0_ffn_weight_int8, L0_ffn_weight_scale, L0_ffn_bias):
    x = np.asarray(L0_attn_x, dtype=np.float32)
    w_inT = np.ascontiguousarray(np.asarray(L0_attn_W_in, dtype=np.float32).T)
    w_outT = np.ascontiguousarray(np.asarray(L0_attn_W_out, dtype=np.float32).T)
    wqT = np.ascontiguousarray(np.asarray(L0_ffn_weight_int8, dtype=np.int8).T)
    cb = np.zeros((17 * 128,), dtype=np.float32)
    cb[:CD] = np.asarray(L0_attn_conv_b, dtype=np.float32)
    cbT = np.ascontiguousarray(cb.reshape(17, 128).T)
    cwp = np.zeros((17 * 128, 4), dtype=np.float32)
    cwp[:CD] = np.asarray(L0_attn_conv_w, dtype=np.float32)
    cwh = np.ascontiguousarray(cwp.reshape(17, 128, 4).transpose(1, 0, 2).reshape(128, 68))
    dtbh = np.ascontiguousarray(np.asarray(L0_attn_dt_bias, dtype=np.float32).reshape(32, 1))
    alogh = np.ascontiguousarray(np.asarray(L0_attn_A_log, dtype=np.float32).reshape(32, 1))
    dvech = np.ascontiguousarray(np.asarray(L0_attn_D, dtype=np.float32).reshape(1, 32))
    nw16h = np.ascontiguousarray(np.asarray(L0_attn_norm_w, dtype=np.float32).reshape(16, 128).T)
    sw8h = np.ascontiguousarray(np.asarray(L0_ffn_weight_scale, dtype=np.float32).reshape(8, 128).T)
    fb8h = np.ascontiguousarray(np.asarray(L0_ffn_bias, dtype=np.float32).reshape(8, 128).T)

    if _NC_CACHE[0] is None:
        _NC_CACHE[0] = build_nc()
    nc = _NC_CACHE[0]

    in_maps = []
    for b in range(8):
        in_maps.append(dict(
            x_fm=np.ascontiguousarray(x[b].T),
            w_inT=w_inT, w_outT=w_outT, wqT=wqT, cbT=cbT, cw=cwh,
            dtb=dtbh, alog=alogh, dvec=dvech, nw16=nw16h, sw8=sw8h, fb8=fb8h,
        ))
    res = run_bass_kernel_spmd(nc, in_maps, core_ids=list(range(8)))
    out = np.empty((8, L, DM), dtype=np.float32)
    for b in range(8):
        out[b] = res.results[b]["out_fm"].T
    return out


if __name__ == "__main__":
    import reference as _r
    inp = _r.setup_inputs()
    o = kernel(**{k: np.asarray(v) for k, v in inp.items()})
    ref = np.asarray(_r.reference(**inp))
    err = np.abs(o - ref)
    rvr = float((err ** 2).mean() / (np.asarray(ref, dtype=np.float64) ** 2).mean())
    print("rvr", rvr, "maxabs", float(err.max()))
